# version 10
# baseline (speedup 1.0000x reference)
import math
from contextlib import ExitStack

import numpy as np
import ml_dtypes

import concourse.bass as bass
import concourse.mybir as mybir
from concourse.bass_utils import run_bass_kernel_spmd

F32 = mybir.dt.float32
BF16 = mybir.dt.bfloat16
AF = mybir.ActivationFunctionType
ALU = mybir.AluOpType
AX = mybir.AxisListType

B, T, D = 2, 4096, 4096
H, KVH, HD = 32, 8, 128
DC = D // 128
TL = 1024
NCORE = 8
SCALE = 1.0 / math.sqrt(HD)
NEGBIG = -30000.0
ENGS = ("sync", "vector", "scalar", "tensor", "gpsimd")


class Sem:

    def __init__(self, st, nc, name):
        self.h = st.enter_context(nc.semaphore(name))
        self.n = 0


class Plan:

    def __init__(self):
        self.q = {e: [] for e in ENGS}

    def op(self, eng, fn, waits=(), inc=None, inck=1):
        waits = [(s.h, v) for (s, v) in waits]
        semh = inc.h if inc is not None else None

        def run(e):
            for h, v in waits:
                e.wait_ge(h, v)
            ins = fn(e)
            if semh is not None:
                ins.then_inc(semh, inck)
        self.q[eng].append(run)
        if inc is not None:
            inc.n += inck
        return inc.n if inc is not None else None

    def wait(self, eng, sem, v):
        self.q[eng].append(lambda e, h=sem.h, v=v: e.wait_ge(h, v))

    def barrier_inc(self, eng, sem):
        self.q[eng].append(lambda e, h=sem.h: e.nop().then_inc(h, 1))
        sem.n += 1


def build_program(phases=4):
    nc = bass.Bass("TRN2", target_bir_lowering=False)

    x_d = nc.dram_tensor("x", [D, TL], F32, kind="ExternalInput").ap()
    wq_d = nc.dram_tensor("wq", [H, 128, D], F32, kind="ExternalInput").ap()
    wk_d = nc.dram_tensor("wk", [KVH, 128, D], F32, kind="ExternalInput").ap()
    wv_d = nc.dram_tensor("wv", [KVH, 128, D], F32, kind="ExternalInput").ap()
    wo_d = nc.dram_tensor("wo", [DC, 128, D], F32, kind="ExternalInput").ap()
    w_d = nc.dram_tensor("wff", [DC, 128, D], F32, kind="ExternalInput").ap()
    w8_d = nc.dram_tensor("w8", [128, DC * 8], F32, kind="ExternalInput").ap()
    a_d = nc.dram_tensor("amat", [128, DC * 32], F32, kind="ExternalInput").ap()
    bmat_d = nc.dram_tensor("bmat", [32, D], F32, kind="ExternalInput").ap()
    lime_d = nc.dram_tensor("lime", [8, 32], F32, kind="ExternalInput").ap()
    limt_d = nc.dram_tensor("limt", [32, 8], F32, kind="ExternalInput").ap()
    b_d = nc.dram_tensor("bvec", [128, DC], F32, kind="ExternalInput").ap()
    b8_d = nc.dram_tensor("b8", [8, 1], F32, kind="ExternalInput").ap()
    cos_d = nc.dram_tensor("cosl", [128, TL], BF16, kind="ExternalInput").ap()
    sin_d = nc.dram_tensor("sinl", [128, TL], BF16, kind="ExternalInput").ap()
    perm_d = nc.dram_tensor("permb", [128, 128], BF16, kind="ExternalInput").ap()
    idb_d = nc.dram_tensor("identb", [128, 128], BF16, kind="ExternalInput").ap()
    idf_d = nc.dram_tensor("identf", [8, 8], F32, kind="ExternalInput").ap()
    mask_d = nc.dram_tensor("masktb", [128, 512], BF16, kind="ExternalInput").ap()
    onesb_d = nc.dram_tensor("onesb", [128, 1], BF16, kind="ExternalInput").ap()
    onesf_d = nc.dram_tensor("onesf", [1, 128], F32, kind="ExternalInput").ap()
    out_d = nc.dram_tensor("out", [D, TL], F32, kind="ExternalOutput").ap()

    kvsrc = nc.dram_tensor("kvsrc", [128, 16384], BF16, kind="Internal").ap()
    kvdst = nc.dram_tensor("kvdst", [512, 16384], BF16, kind="Internal").ap()
    mxsrc = nc.dram_tensor("mxsrc", [1, 4], F32, kind="Internal").ap()
    mxdst = nc.dram_tensor("mxdst", [1, 4], F32, kind="Internal").ap()
    q_dram = nc.dram_tensor("q_dram", [H, 128, TL], BF16, kind="Internal").ap()

    G4 = [[0, 1, 2, 3], [4, 5, 6, 7]]
    G8 = [[0, 1, 2, 3, 4, 5, 6, 7]]

    P = Plan()

    with ExitStack() as st:
        sbt = lambda *a: st.enter_context(nc.sbuf_tensor(*a))
        xn = sbt("xn", [128, DC * TL], BF16)
        osb = sbt("osb", [128, H * TL], BF16)
        idb = sbt("idb", [128, 128], BF16)
        idf = sbt("idf", [8, 8], F32)
        onesb = sbt("onesb_s", [128, 1], BF16)
        onesf = sbt("onesf_s", [1, 128], F32)
        masktb = sbt("masktb_s", [128, 512], BF16)
        bsb = sbt("bsb", [128, DC], F32)
        zpbf = sbt("zpbf", [32, TL], BF16)
        zf = sbt("zf", [32, TL], F32)
        bmb = sbt("bmb", [32, D], BF16)
        negb = sbt("negb", [128, 1], F32)
        mx4 = sbt("mx4", [1, 4], F32)
        mxg = sbt("mxg", [1, 4], F32)

        S = {}
        for name in ("c", "xl", "v0", "t0", "s0", "b01", "wl", "wc", "kv", "r1",
                     "e1", "g1", "tp", "cc", "q2", "v2", "s2", "t2", "gq", "rt",
                     "rs", "rr", "ar", "b23", "kq", "ql", "ts", "se", "tv",
                     "sl", "tb", "sr", "vo", "b34", "wl4", "wc4", "x4", "t4",
                     "s4", "v4", "o4"):
            S[name] = Sem(st, nc, "s_" + name)

        blk = st.enter_context(nc.Block())

        st0 = st.enter_context(ExitStack())
        sb0 = lambda *a: st0.enter_context(nc.sbuf_tensor(*a))
        xf = [sb0(f"xf{i}", [128, TL], F32) for i in range(2)]
        x2f = sb0("x2f", [128, TL], F32)
        xb2 = [sb0(f"xb2{i}", [128, TL], BF16) for i in range(2)]
        x2b = [sb0(f"x2b{i}", [128, TL], BF16) for i in range(2)]
        srow = sb0("srow", [1, 5 * TL], F32)
        ps0 = lambda n: st0.enter_context(nc.psum_tensor(n, [128, 512], F32))
        msum = [ps0(f"msum{i}") for i in range(2)]
        esum = [ps0(f"esum{i}") for i in range(2)]
        mbc = [ps0(f"mbc{i}") for i in range(2)]
        rbc = [ps0(f"rbc{i}") for i in range(2)]
        NEGM, MSQ, VAR, IVAR, RSTD = (srow[0:1, i * TL:(i + 1) * TL] for i in range(5))

        for src, dst in [(idb_d, idb), (idf_d, idf), (onesb_d, onesb),
                         (onesf_d, onesf), (mask_d, masktb), (b_d, bsb)]:
            P.op("sync", lambda e, s=src, d=dst: e.dma_start(out=d[:], in_=s[:, :]),
                 inc=S["c"], inck=16)
        NCONST0 = S["c"].n

        pass1v = {}
        stats_t = {}
        for dc in range(DC):
            p = dc % 2
            w = [(S["v0"], pass1v[dc - 2])] if dc >= 2 else []
            P.op("sync", lambda e, p=p, dc=dc: e.dma_start(
                out=xf[p][:], in_=x_d[dc * 128:(dc + 1) * 128, :]),
                waits=w, inc=S["xl"], inck=16)

            w = [(S["xl"], 16 * (dc + 1))]
            if dc == 0:
                w.append((S["c"], NCONST0))
            if dc >= 2:
                w.append((S["t0"], stats_t[dc - 2]))

            def f_v(e, p=p):
                e.tensor_tensor(x2f[:], xf[p][:], xf[p][:], ALU.mult)
                e.tensor_copy(xb2[p][:], xf[p][:])
                return e.tensor_copy(x2b[p][:], x2f[:])
            pass1v[dc] = P.op("vector", f_v, waits=w, inc=S["v0"])

            def f_t(e, p=p, dc=dc):
                st_, sp_ = (dc == 0), (dc == DC - 1)
                e.matmul(msum[0][0:1, :], onesb[:], xb2[p][:, 0:512], start=st_, stop=sp_)
                e.matmul(msum[1][0:1, :], onesb[:], xb2[p][:, 512:1024], start=st_, stop=sp_)
                e.matmul(esum[0][0:1, :], onesb[:], x2b[p][:, 0:512], start=st_, stop=sp_)
                return e.matmul(esum[1][0:1, :], onesb[:], x2b[p][:, 512:1024],
                                start=st_, stop=sp_)
            stats_t[dc] = P.op("tensor", f_t, waits=[(S["v0"], pass1v[dc])], inc=S["t0"])

        def f(e):
            e.tensor_scalar(NEGM[:, 0:512], msum[0][0:1, :], -1.0 / D, None, ALU.mult)
            return e.tensor_scalar(NEGM[:, 512:1024], msum[1][0:1, :], -1.0 / D,
                                   None, ALU.mult)
        n_negm = P.op("vector", f, waits=[(S["t0"], stats_t[DC - 1])], inc=S["v0"])

        n_msq = P.op("scalar", lambda e: e.activation(MSQ[:], NEGM[:], AF.Square),
                     waits=[(S["v0"], n_negm)], inc=S["s0"])

        def f(e):
            e.scalar_tensor_tensor(VAR[:, 0:512], esum[0][0:1, :], 1.0 / D,
                                   MSQ[:, 0:512], ALU.mult, ALU.subtract)
            e.scalar_tensor_tensor(VAR[:, 512:1024], esum[1][0:1, :], 1.0 / D,
                                   MSQ[:, 512:1024], ALU.mult, ALU.subtract)
            e.tensor_scalar(VAR[:], VAR[:], 1e-5, None, ALU.add)
            return e.reciprocal(IVAR[:], VAR[:])
        n_ivar = P.op("vector", f, waits=[(S["s0"], n_msq)], inc=S["v0"])

        n_rstd = P.op("scalar", lambda e: e.activation(RSTD[:], IVAR[:], AF.Sqrt),
                      waits=[(S["v0"], n_ivar)], inc=S["s0"])

        def f(e):
            e.matmul(mbc[0][:], onesf[:], NEGM[:, 0:512], start=True, stop=True)
            e.matmul(mbc[1][:], onesf[:], NEGM[:, 512:1024], start=True, stop=True)
            e.matmul(rbc[0][:], onesf[:], RSTD[:, 0:512], start=True, stop=True)
            return e.matmul(rbc[1][:], onesf[:], RSTD[:, 512:1024], start=True, stop=True)
        n_bc0 = P.op("tensor", f, waits=[(S["s0"], n_rstd)], inc=S["t0"])

        xn_v = {}
        for dc in range(DC):
            p = dc % 2
            if dc < 2:
                w = [(S["v0"], n_ivar)]
            else:
                w = [(S["v0"], xn_v[dc - 2])]
            P.op("sync", lambda e, p=p, dc=dc: e.dma_start(
                out=xf[p][:], in_=x_d[dc * 128:(dc + 1) * 128, :]),
                waits=w, inc=S["xl"], inck=16)

            w = [(S["xl"], 16 * (DC + dc + 1))]
            if dc == 0:
                w.append((S["t0"], n_bc0))

            def f_v(e, p=p, dc=dc):
                for hh in range(2):
                    c0, c1 = hh * 512, (hh + 1) * 512
                    e.tensor_tensor(x2f[:, c0:c1], xf[p][:, c0:c1], mbc[hh][:], ALU.add)
                    ins = e.tensor_tensor(xn[:, dc * TL + c0:dc * TL + c1],
                                          x2f[:, c0:c1], rbc[hh][:], ALU.mult)
                return ins
            xn_v[dc] = P.op("vector", f_v, waits=w, inc=S["v0"])
        XN_DONE = S["v0"].n

        if phases <= 0:
            P.op("gpsimd", lambda e: e.dma_start(out=out_d[0:128, :], in_=xn[:, 0:TL]),
                 waits=[(S["v0"], XN_DONE)], inc=S["o4"], inck=16)
            P.wait("gpsimd", S["o4"], 16)
            return _finish(nc, P, blk, st)
        P.barrier_inc("vector", S["b01"])
        P.barrier_inc("tensor", S["b01"])
        P.barrier_inc("scalar", S["b01"])
        st0.close()

        st1 = st.enter_context(ExitStack())
        sb1 = lambda *a: st1.enter_context(nc.sbuf_tensor(*a))
        cosl = sb1("cosl_s", [128, TL], BF16)
        sinl = sb1("sinl_s", [128, TL], BF16)
        permb = sb1("permb_s", [128, 128], BF16)
        wf32 = sb1("wf32", [128, D], F32)
        wbf = [sb1(f"wbf{i}", [128, D], BF16) for i in range(2)]
        qs_b = sb1("qs_b", [128, TL], BF16)
        qc_b = sb1("qc_b", [128, TL], BF16)
        ktmp = [sb1(f"ktmp{i}", [128, TL], BF16) for i in range(2)]
        vtmp0 = sb1("vtmp0", [128, TL], BF16)
        vtmp = [vtmp0, vtmp0]
        vttmp0 = sb1("vttmp0", [128, TL], BF16)
        vttmp = [vttmp0, vttmp0]
        sqb = sb1("sqb", [128, TL], BF16)
        qsb = [sb1(f"qsb{i}", [128, TL], BF16) for i in range(2)]
        abf = sb1("abf", [128, DC * 32], BF16)
        w8b = sb1("w8b", [128, DC * 8], BF16)
        limeb = sb1("limeb", [8, 32], BF16)
        limtb = sb1("limtb", [32, 8], BF16)
        limef = sb1("limef", [8, 32], F32)
        limtf = sb1("limtf", [32, 8], F32)
        b8f = sb1("b8f", [8, 1], F32)
        hslf = sb1("hslf", [8, 512], F32)
        dslf = sb1("dslf", [8, 512], F32)
        rt8 = sb1("rt8", [8, 2], F32)
        lsb = sb1("lsb", [128, 32], F32)
        expv = sb1("expv", [128, 32], F32)
        probs = sb1("probs", [128, 32], F32)
        mskv = sb1("mskv", [128, 32], F32)
        wnb = sb1("wnb", [128, 32], BF16)
        rcol = sb1("rcol", [128, 32], F32)
        tiny = sb1("tiny", [1, 16], F32)
        we_b = sb1("we_b", [8, 512], BF16)
        knm = sb1("knm", [1, 16], F32)
        qnm = sb1("qnm", [1, 64], F32)
        rHbc = sb1("rHbc", [128, 1], F32)
        rDbc = sb1("rDbc", [128, 1], F32)
        ps1 = lambda n, dt=F32: st1.enter_context(nc.psum_tensor(n, [128, 512], dt))
        kvp = [[ps1(f"kvp{i}{j}") for j in range(2)] for i in range(2)]
        rop = [ps1(f"rop{j}") for j in range(2)]
        vtp = st1.enter_context(nc.psum_tensor("vtp", [128, 1024], BF16))

        first = True
        for src, dst in [(cos_d, cosl), (sin_d, sinl), (perm_d, permb),
                         (lime_d, limef), (limt_d, limtf), (b8_d, b8f)]:
            w = [(S["b01"], 3)] if first else []
            first = False
            P.op("sync", lambda e, s=src, d=dst: e.dma_start(out=d[:], in_=s[:, :]),
                 waits=w, inc=S["c"], inck=16)

        def f(e):
            e.tensor_copy(limeb[:], limef[:])
            e.memset(mx4[:], 0.0)
            return e.tensor_copy(limtb[:], limtf[:])
        P.op("vector", f, waits=[(S["c"], S["c"].n)], inc=S["wc"])

        P.op("sync", lambda e: e.dma_start(out=wf32[0:32, :], in_=bmat_d[:, :]),
             waits=[(S["b01"], 3)], inc=S["c"], inck=16)
        P.op("vector", lambda e: e.tensor_scalar(bmb[:], wf32[0:32, :], 0.5,
                                                 None, ALU.mult),
             waits=[(S["c"], S["c"].n)], inc=S["wc"])
        P.op("sync", lambda e: e.dma_start(out=wf32[:, 0:DC * 32], in_=a_d[:, :]),
             waits=[(S["wc"], S["wc"].n)], inc=S["c"], inck=16)
        P.op("vector", lambda e: e.tensor_copy(abf[:], wf32[:, 0:DC * 32]),
             waits=[(S["c"], S["c"].n)], inc=S["wc"])
        P.op("sync", lambda e: e.dma_start(out=wf32[:, 0:DC * 8], in_=w8_d[:, :]),
             waits=[(S["wc"], S["wc"].n)], inc=S["c"], inck=16)
        P.op("vector", lambda e: e.tensor_copy(w8b[:], wf32[:, 0:DC * 8]),
             waits=[(S["c"], S["c"].n)], inc=S["wc"])
        CONST_CAST = S["wc"].n

        kv_done = {}
        kvp_free = {}
        perm_done = {}
        evk_done = {}
        sq_done = {}
        bnd_done = {}
        bndv_done = {}
        rv_done = {}
        ev_done = {}
        tpv_done = {}
        vte_done = {}
        g1_done = {}
        wc_done = {}

        def plan_wstream(n_global, dram_src, extra_sync_waits=(), consumer_free=None):
            w = list(extra_sync_waits)
            w.append((S["wc"], CONST_CAST + n_global))
            P.op("sync", lambda e, s=dram_src: e.dma_start(out=wf32[:], in_=s),
                 waits=w, inc=S["wl"], inck=16)
            w = [(S["wl"], 16 * (n_global + 1))]
            if consumer_free is not None:
                w.append(consumer_free)
            wc_done[n_global] = P.op(
                "vector", lambda e, p=n_global % 2: e.tensor_copy(wbf[p][:], wf32[:]),
                waits=w, inc=S["wc"])

        NKV = 16
        for n in range(NKV):
            p = n % 2
            src = (wk_d[n] if n < KVH else wv_d[n - KVH])[:, :]
            plan_wstream(n, src,
                         consumer_free=((S["kv"], kv_done[n - 2]) if n >= 2 else None))

            w = [(S["wc"], CONST_CAST + n + 1)]
            if n == 0:
                w.append((S["v0"], XN_DONE))
            if n >= 2:
                w.append(kvp_free[n - 2])

            def f_t(e, p=p):
                for dc in range(DC):
                    st_, sp_ = (dc == 0), (dc == DC - 1)
                    lw = wbf[p][:, dc * 128:(dc + 1) * 128]
                    e.matmul(kvp[p][0][:], lw, xn[:, dc * TL:dc * TL + 512],
                             start=st_, stop=sp_)
                    ins = e.matmul(kvp[p][1][:], lw, xn[:, dc * TL + 512:(dc + 1) * TL],
                                   start=st_, stop=sp_)
                return ins
            kv_done[n] = P.op("tensor", f_t, waits=w, inc=S["kv"])

            if n < KVH:
                w = [(S["kv"], kv_done[n])]
                if n >= 1:
                    w.append((S["tp"], perm_done[n - 1]))

                def f_v(e, p=p):
                    for hh in range(2):
                        c0, c1 = hh * 512, (hh + 1) * 512
                        e.tensor_tensor(qs_b[:, c0:c1], kvp[p][hh][:],
                                        sinl[:, c0:c1], ALU.mult)
                        ins = e.tensor_tensor(qc_b[:, c0:c1], kvp[p][hh][:],
                                              cosl[:, c0:c1], ALU.mult)
                    return ins
                rv_done[n] = P.op("vector", f_v, waits=w, inc=S["r1"])
                kvp_free[n] = (S["r1"], rv_done[n])

                w = [(S["r1"], rv_done[n])]
                if n >= 1:
                    w.append((S["r1"], bndv_done[n - 1]))

                def f_t(e):
                    for hh in range(2):
                        c0, c1 = hh * 512, (hh + 1) * 512
                        e.matmul(rop[hh][:], permb[:], qs_b[:, c0:c1],
                                 start=True, stop=False)
                        ins = e.matmul(rop[hh][:], idb[:], qc_b[:, c0:c1],
                                       start=False, stop=True)
                    return ins
                perm_done[n] = P.op("tensor", f_t, waits=w, inc=S["tp"])

                w = [(S["tp"], perm_done[n])]
                if n >= 2:
                    w.append((S["g1"], g1_done[n - 2]))

                def f_s(e, p=p):
                    e.activation(ktmp[p][:, 0:512], rop[0][:], AF.Copy)
                    return e.activation(ktmp[p][:, 512:1024], rop[1][:], AF.Copy)
                evk_done[n] = P.op("scalar", f_s, waits=w, inc=S["e1"])

                w = []
                if n >= 1:
                    w.append((S["tp"], bnd_done[n - 1]))
                sq_done[n] = P.op(
                    "scalar", lambda e, p=p: e.activation(sqb[:], ktmp[p][:], AF.Square),
                    waits=w, inc=S["e1"])

                def f_t(e):
                    e.matmul(rop[0][0:1, :], onesb[:], sqb[:, 0:512], start=True, stop=True)
                    return e.matmul(rop[1][0:1, :], onesb[:], sqb[:, 512:1024],
                                    start=True, stop=True)
                bnd_done[n] = P.op("tensor", f_t, waits=[(S["e1"], sq_done[n])],
                                   inc=S["tp"])

                def f_v(e, n=n):
                    e.reduce_max(knm[0:1, 2 * n:2 * n + 1], rop[0][0:1, :], axis=AX.X)
                    return e.reduce_max(knm[0:1, 2 * n + 1:2 * n + 2], rop[1][0:1, :],
                                        axis=AX.X)
                bndv_done[n] = P.op("vector", f_v, waits=[(S["tp"], bnd_done[n])],
                                    inc=S["r1"])

                g1_done[n] = P.op("gpsimd", lambda e, p=p, n=n: e.dma_start(
                    out=kvsrc[:, n * TL:(n + 1) * TL], in_=ktmp[p][:]),
                    waits=[(S["e1"], evk_done[n])], inc=S["g1"], inck=16)
            else:
                c = n - KVH
                w = [(S["kv"], kv_done[n])]
                if n - 1 >= KVH:
                    w.append((S["tp"], tpv_done[n - 1]))

                def f_s(e, p=p):
                    e.activation(vtmp[p][:, 0:512], kvp[p][0][:], AF.Copy)
                    return e.activation(vtmp[p][:, 512:1024], kvp[p][1][:], AF.Copy)
                ev_done[n] = P.op("scalar", f_s, waits=w, inc=S["e1"])
                kvp_free[n] = (S["e1"], ev_done[n])

                w = [(S["e1"], ev_done[n])]
                if n - 2 >= KVH:
                    w.append((S["r1"], vte_done[n - 2]))

                def f_t(e, p=p):
                    for g in range(8):
                        ins = e.transpose(vtp[:, g * 128:(g + 1) * 128],
                                          vtmp[p][:, g * 128:(g + 1) * 128], idb[:])
                    return ins
                tpv_done[n] = P.op("tensor", f_t, waits=w, inc=S["tp"])

                w = [(S["tp"], tpv_done[n])]
                if n - 1 >= KVH:
                    w.append((S["g1"], g1_done[n - 1]))
                vte_done[n] = P.op(
                    "vector", lambda e, p=p: e.tensor_copy(vttmp[p][:], vtp[:]),
                    waits=w, inc=S["r1"])

                dst = kvsrc[:, 8192:16384].rearrange(
                    "p (g c m) -> p g c m", g=8, c=8)[:, :, c, :]
                srcap = vttmp[p][:].rearrange("p (g m) -> p g m", g=8)
                g1_done[n] = P.op("gpsimd", lambda e, d=dst, s=srcap: e.dma_start(
                    out=d, in_=s), waits=[(S["r1"], vte_done[n])],
                    inc=S["g1"], inck=16)

        n_ag = P.op("gpsimd", lambda e: e.collective_compute(
            "AllGather", ALU.bypass, replica_groups=G4,
            ins=[kvsrc.opt()], outs=[kvdst.opt()]),
            waits=[(S["g1"], 16 * NKV)], inc=S["cc"])

        if phases == 1:
            P.op("gpsimd", lambda e: e.dma_start(out=out_d[0:128, :],
                                                 in_=ktmp[1][:]),
                 waits=[(S["cc"], n_ag)], inc=S["o4"], inck=16)
            P.wait("gpsimd", S["o4"], 16)
            return _finish(nc, P, blk, st)
        q_done = {}
        rv2 = {}
        perm2 = {}
        ev2 = {}
        sq2 = {}
        bnd2 = {}
        bndv2 = {}
        gq_done = {}
        for h in range(H):
            p = h % 2
            ng = NKV + h
            plan_wstream(ng, wq_d[h][:, :],
                         consumer_free=((S["q2"], q_done[h - 2]) if h >= 2
                                        else (S["kv"], kv_done[NKV - 2 + h])))

            w = [(S["wc"], CONST_CAST + ng + 1)]
            if h < 2:
                w.append(kvp_free[NKV - 2 + h])
            else:
                w.append((S["v2"], rv2[h - 2]))

            def f_t(e, p=p):
                for dc in range(DC):
                    st_, sp_ = (dc == 0), (dc == DC - 1)
                    lw = wbf[ (NKV + 0) % 2 ] if False else None
                    return None
            def f_t(e, p=p, ng=ng):
                wb = wbf[ng % 2]
                for dc in range(DC):
                    st_, sp_ = (dc == 0), (dc == DC - 1)
                    lw = wb[:, dc * 128:(dc + 1) * 128]
                    e.matmul(kvp[p][0][:], lw, xn[:, dc * TL:dc * TL + 512],
                             start=st_, stop=sp_)
                    ins = e.matmul(kvp[p][1][:], lw, xn[:, dc * TL + 512:(dc + 1) * TL],
                                   start=st_, stop=sp_)
                return ins
            q_done[h] = P.op("tensor", f_t, waits=w, inc=S["q2"])

            w = [(S["q2"], q_done[h])]
            if h == 0:
                w.append((S["tp"], perm_done[KVH - 1]))
            else:
                w.append((S["t2"], perm2[h - 1]))

            def f_v(e, p=p):
                for hh in range(2):
                    c0, c1 = hh * 512, (hh + 1) * 512
                    e.tensor_tensor(qs_b[:, c0:c1], kvp[p][hh][:], sinl[:, c0:c1], ALU.mult)
                    ins = e.tensor_tensor(qc_b[:, c0:c1], kvp[p][hh][:],
                                          cosl[:, c0:c1], ALU.mult)
                return ins
            rv2[h] = P.op("vector", f_v, waits=w, inc=S["v2"])

            w = [(S["v2"], rv2[h])]
            if h == 0:
                w.append((S["r1"], bndv_done[KVH - 1]))
            else:
                w.append((S["v2"], bndv2[h - 1]))

            def f_t(e):
                for hh in range(2):
                    c0, c1 = hh * 512, (hh + 1) * 512
                    e.matmul(rop[hh][:], permb[:], qs_b[:, c0:c1], start=True, stop=False)
                    ins = e.matmul(rop[hh][:], idb[:], qc_b[:, c0:c1],
                                   start=False, stop=True)
                return ins
            perm2[h] = P.op("tensor", f_t, waits=w, inc=S["t2"])

            w = [(S["t2"], perm2[h])]
            if h >= 2:
                w.append((S["gq"], gq_done[h - 2]))

            def f_s(e, p=p):
                e.activation(qsb[p][:, 0:512], rop[0][:], AF.Copy)
                return e.activation(qsb[p][:, 512:1024], rop[1][:], AF.Copy)
            ev2[h] = P.op("scalar", f_s, waits=w, inc=S["s2"])

            w = []
            if h == 0:
                w.append((S["tp"], bnd_done[KVH - 1]))
            else:
                w.append((S["t2"], bnd2[h - 1]))
            sq2[h] = P.op("scalar",
                          lambda e, p=p: e.activation(sqb[:], qsb[p][:], AF.Square),
                          waits=w, inc=S["s2"])

            w = [(S["s2"], sq2[h])]

            def f_t(e):
                e.matmul(rop[0][0:1, :], onesb[:], sqb[:, 0:512], start=True, stop=True)
                return e.matmul(rop[1][0:1, :], onesb[:], sqb[:, 512:1024],
                                start=True, stop=True)
            bnd2[h] = P.op("tensor", f_t, waits=w, inc=S["t2"])

            def f_v(e, h=h):
                e.reduce_max(qnm[0:1, 2 * h:2 * h + 1], rop[0][0:1, :], axis=AX.X)
                return e.reduce_max(qnm[0:1, 2 * h + 1:2 * h + 2], rop[1][0:1, :],
                                    axis=AX.X)
            bndv2[h] = P.op("vector", f_v, waits=[(S["t2"], bnd2[h])], inc=S["v2"])

            gq_done[h] = P.op("gpsimd", lambda e, p=p, h=h: e.dma_start(
                out=q_dram[h][:, :], in_=qsb[p][:]),
                waits=[(S["s2"], ev2[h])], inc=S["gq"], inck=16)

        w = [(S["v2"], rv2[31]), (S["wc"], CONST_CAST)]

        def f_t(e):
            for dc in range(DC):
                st_, sp_ = (dc == 0), (dc == DC - 1)
                la = abf[:, dc * 32:(dc + 1) * 32]
                e.matmul(kvp[1][0][0:32, :], la, xn[:, dc * TL:dc * TL + 512],
                         start=st_, stop=sp_)
                ins = e.matmul(kvp[1][1][0:32, :], la,
                               xn[:, dc * TL + 512:(dc + 1) * TL], start=st_, stop=sp_)
            return ins
        n_z = P.op("tensor", f_t, waits=w, inc=S["rr"])

        w = [(S["v2"], bndv2[31])]

        def f_t(e):
            for dc in range(DC):
                st_, sp_ = (dc == 0), (dc == DC - 1)
                lw = w8b[:, dc * 8:(dc + 1) * 8]
                e.matmul(rop[0][0:8, :], lw, xn[:, dc * TL:dc * TL + 512],
                         start=st_, stop=sp_)
                ins = e.matmul(rop[1][0:8, :], lw, xn[:, dc * TL + 512:(dc + 1) * TL],
                               start=st_, stop=sp_)
            return ins
        n_h8 = P.op("tensor", f_t, waits=w, inc=S["rr"])

        def f_v(e):
            e.tensor_copy(zf[:, 0:512], kvp[1][0][0:32, :])
            e.tensor_copy(zf[:, 512:1024], kvp[1][1][0:32, :])
            zv = zf[:].rearrange("p (a two) -> p two a", two=2)[:, 1, :]
            return e.tensor_copy(qs_b[0:32, 0:512], zv)
        n_zev = P.op("vector", f_v, waits=[(S["rr"], n_z), (S["t2"], perm2[31])],
                     inc=S["rt"])

        n_dsl = P.op("tensor", lambda e: e.matmul(
            kvp[1][0][0:8, :], limtb[:], qs_b[0:32, 0:512], start=True, stop=True),
            waits=[(S["rt"], n_zev)], inc=S["rr"])

        def f_v(e):
            h0 = rop[0][0:8, :].rearrange("p (a two) -> p two a", two=2)[:, 1, :]
            h1 = rop[1][0:8, :].rearrange("p (a two) -> p two a", two=2)[:, 1, :]
            e.tensor_scalar(hslf[:, 0:256], h0, b8f[:], None, ALU.add)
            return e.tensor_scalar(hslf[:, 256:512], h1, b8f[:], None, ALU.add)
        n_hsl = P.op("vector", f_v, waits=[(S["rr"], n_h8)], inc=S["rt"])

        n_dslf = P.op("vector", lambda e: e.tensor_copy(dslf[:], kvp[1][0][0:8, :]),
                      waits=[(S["rr"], n_dsl)], inc=S["rt"])

        def f_v(e):
            e.reduce_max(rt8[:, 0:1], hslf[:], axis=AX.X, apply_absolute_value=True)
            return e.reduce_max(rt8[:, 1:2], dslf[:], axis=AX.X,
                                apply_absolute_value=True)
        n_rt8 = P.op("vector", f_v, waits=[(S["rt"], n_dslf)], inc=S["rt"])

        def f_t(e):
            e.transpose(kvp[0][0][0:1, 0:8], rt8[0:8, 0:1], idf[0:8, 0:8])
            return e.transpose(kvp[0][0][0:1, 8:16], rt8[0:8, 1:2], idf[0:8, 0:8])
        n_tr8 = P.op("tensor", f_t, waits=[(S["rt"], n_rt8), (S["v2"], bndv2[30])],
                     inc=S["rr"])

        def f_v(e):
            e.reduce_max(mx4[0:1, 0:1], kvp[0][0][0:1, 0:8], axis=AX.X)
            e.reduce_max(mx4[0:1, 1:2], kvp[0][0][0:1, 8:16], axis=AX.X)
            e.reduce_max(mx4[0:1, 2:3], qnm[0:1, :], axis=AX.X)
            return e.reduce_max(mx4[0:1, 3:4], knm[0:1, :], axis=AX.X)
        n_mx = P.op("vector", f_v,
                    waits=[(S["rr"], n_tr8), (S["v2"], bndv2[31]),
                           (S["r1"], bndv_done[KVH - 1])], inc=S["rt"])

        P.op("gpsimd", lambda e: e.dma_start(out=mxsrc[:, :], in_=mx4[:]),
             waits=[(S["rt"], n_mx)], inc=S["ar"], inck=16)
        P.op("gpsimd", lambda e: e.collective_compute(
            "AllReduce", ALU.max, replica_groups=G8,
            ins=[mxsrc.opt()], outs=[mxdst.opt()]),
            waits=[(S["ar"], 16)], inc=S["ar"])
        P.op("gpsimd", lambda e: e.dma_start(out=mxg[:], in_=mxdst[:, :]),
             waits=[(S["ar"], 17)], inc=S["ar"], inck=16)
        AR_DONE = S["ar"].n

        def f_v(e):
            e.tensor_scalar_max(tiny[0:1, 0:1], mxg[0:1, 0:1], 1e-6)
            e.tensor_scalar_max(tiny[0:1, 1:2], mxg[0:1, 1:2], 1e-6)
            return e.tensor_tensor(tiny[0:1, 4:5], mxg[0:1, 2:3], mxg[0:1, 3:4], ALU.mult)
        n_sc0 = P.op("vector", f_v, waits=[(S["ar"], AR_DONE)], inc=S["rt"])

        def f_v(e):
            e.reciprocal(tiny[0:1, 8:9], tiny[0:1, 0:1])
            return e.reciprocal(tiny[0:1, 9:10], tiny[0:1, 1:2])
        n_sc1 = P.op("vector", f_v, waits=[(S["rt"], n_sc0)], inc=S["rt"])

        def f_v(e):
            e.tensor_scalar(tiny[0:1, 2:3], tiny[0:1, 8:9], 0.5, None, ALU.mult)
            return e.tensor_scalar(tiny[0:1, 3:4], tiny[0:1, 9:10], 0.5, None, ALU.mult)
        n_sc = P.op("vector", f_v, waits=[(S["rt"], n_sc1)], inc=S["rt"])

        n_sq = P.op("scalar", lambda e: e.activation(tiny[0:1, 5:6], tiny[0:1, 4:5],
                                                     AF.Sqrt),
                    waits=[(S["rt"], n_sc)], inc=S["rs"])
        n_nbA = P.op("vector", lambda e: e.tensor_scalar(
            tiny[0:1, 10:11], tiny[0:1, 5:6], -1.01 * SCALE, 75.0, ALU.mult, ALU.add),
            waits=[(S["rs"], n_sq)], inc=S["rt"])
        n_nb1 = P.op("vector", lambda e: e.tensor_scalar_min(
            tiny[0:1, 6:7], tiny[0:1, 10:11], 0.0),
            waits=[(S["rt"], n_nbA)], inc=S["rt"])

        def f_t(e):
            e.matmul(kvp[0][1][:, 0:1], onesf[:], tiny[0:1, 2:3], start=True, stop=True)
            e.matmul(kvp[0][1][:, 1:2], onesf[:], tiny[0:1, 3:4], start=True, stop=True)
            return e.matmul(kvp[0][1][:, 2:3], onesf[:], tiny[0:1, 6:7],
                            start=True, stop=True)
        n_bcr = P.op("tensor", f_t, waits=[(S["rt"], n_nb1)], inc=S["rr"])

        def f_v(e):
            e.tensor_copy(rHbc[:], kvp[0][1][:, 0:1])
            e.tensor_copy(rDbc[:], kvp[0][1][:, 1:2])
            return e.tensor_copy(negb[:], kvp[0][1][:, 2:3])
        n_bcv = P.op("vector", f_v, waits=[(S["rr"], n_bcr)], inc=S["rt"])

        def f_t(e):
            for t in range(4):
                e.transpose(rop[0][:, t * 8:(t + 1) * 8],
                            hslf[0:8, t * 128:(t + 1) * 128], idf[0:8, 0:8])
            for t in range(4):
                ins = e.transpose(rop[0][:, 32 + t * 8:32 + (t + 1) * 8],
                                  dslf[0:8, t * 128:(t + 1) * 128], idf[0:8, 0:8])
            return ins
        n_trl = P.op("tensor", f_t, waits=[(S["rt"], n_rt8)], inc=S["rr"])

        def f_v(e):
            e.wait_ge(S["rt"].h, n_bcv)
            e.tensor_scalar(expv[:], rop[0][:, 0:32], rHbc[:], None, ALU.mult)
            return e.scalar_tensor_tensor(lsb[:], rop[0][:, 32:64], rDbc[:],
                                          expv[:], ALU.mult, ALU.add)
        n_log = P.op("vector", f_v, waits=[(S["rr"], n_trl)], inc=S["rt"])

        def f_v(e):
            for t in range(4):
                ins = e.reduce_max(rcol[:, t:t + 1], lsb[:, t * 8:(t + 1) * 8],
                                   axis=AX.X, negate=True)
            return ins
        n_nm = P.op("vector", f_v, waits=[(S["rt"], n_log)], inc=S["rt"])

        def f_s(e):
            for t in range(4):
                ins = e.activation(expv[:, t * 8:(t + 1) * 8], lsb[:, t * 8:(t + 1) * 8],
                                   AF.Exp, bias=rcol[:, t:t + 1], scale=1.0)
            return ins
        n_exp = P.op("scalar", f_s, waits=[(S["rt"], n_nm)], inc=S["rs"])

        def f_v(e):
            for t in range(4):
                ins = e.reduce_sum(rcol[:, 4 + t:5 + t], expv[:, t * 8:(t + 1) * 8],
                                   axis=AX.X)
            return ins
        n_rs0 = P.op("vector", f_v, waits=[(S["rs"], n_exp)], inc=S["rt"])
        n_rsum = P.op("vector", lambda e: e.reciprocal(rcol[:, 8:12], rcol[:, 4:8]),
                      waits=[(S["rt"], n_rs0)], inc=S["rt"])

        def f_v(e):
            for t in range(4):
                e.tensor_scalar(probs[:, t * 8:(t + 1) * 8], expv[:, t * 8:(t + 1) * 8],
                                rcol[:, 8 + t:9 + t], None, ALU.mult)
            for t in range(4):
                ins = e.reduce_max(rcol[:, 12 + t:13 + t], probs[:, t * 8:(t + 1) * 8],
                                   axis=AX.X)
            return ins
        n_m1 = P.op("vector", f_v, waits=[(S["rt"], n_rsum)], inc=S["rt"])

        def f_v(e):
            for t in range(4):
                e.tensor_scalar(mskv[:, t * 8:(t + 1) * 8], probs[:, t * 8:(t + 1) * 8],
                                rcol[:, 12 + t:13 + t], -1e30, ALU.is_equal, ALU.mult)
            e.tensor_tensor(mskv[:], mskv[:], probs[:], ALU.add)
            for t in range(4):
                ins = e.reduce_max(rcol[:, 16 + t:17 + t], mskv[:, t * 8:(t + 1) * 8],
                                   axis=AX.X)
            return ins
        n_m2a = P.op("vector", f_v, waits=[(S["rt"], n_m1)], inc=S["rt"])
        n_m2 = P.op("vector", lambda e: e.tensor_scalar(rcol[:, 20:24], rcol[:, 16:20],
                                                        -2.0, None, ALU.mult),
                    waits=[(S["rt"], n_m2a)], inc=S["rt"])

        def f_s(e):
            for t in range(4):
                ins = e.activation(mskv[:, t * 8:(t + 1) * 8], probs[:, t * 8:(t + 1) * 8],
                                   AF.Sigmoid, bias=rcol[:, 20 + t:21 + t], scale=2.0)
            return ins
        n_sig = P.op("scalar", f_s, waits=[(S["rt"], n_m2)], inc=S["rs"])

        def f_v(e):
            e.tensor_tensor(expv[:], probs[:], mskv[:], ALU.mult)
            for t in range(4):
                ins = e.reduce_sum(rcol[:, 24 + t:25 + t], expv[:, t * 8:(t + 1) * 8],
                                   axis=AX.X)
            return ins
        n_rw0 = P.op("vector", f_v, waits=[(S["rs"], n_sig)], inc=S["rt"])

        def f_v(e):
            e.tensor_scalar(rcol[:, 24:28], rcol[:, 24:28], 1e-9, None, ALU.add)
            return e.reciprocal(rcol[:, 28:32], rcol[:, 24:28])
        n_rws = P.op("vector", f_v, waits=[(S["rt"], n_rw0)], inc=S["rt"])

        def f_v(e):
            for t in range(4):
                ins = e.tensor_scalar(wnb[:, t * 8:(t + 1) * 8], expv[:, t * 8:(t + 1) * 8],
                                      rcol[:, 28 + t:29 + t], None, ALU.mult)
            return ins
        n_wn = P.op("vector", f_v, waits=[(S["rt"], n_rws)], inc=S["rt"])

        def f_t(e):
            for t in range(4):
                ins = e.transpose(vtp[0:8, t * 128:(t + 1) * 128],
                                  wnb[:, t * 8:(t + 1) * 8], idb[:])
            return ins
        n_wt = P.op("tensor", f_t, waits=[(S["rt"], n_wn),
                                          (S["r1"], vte_done[NKV - 1])],
                    inc=S["rr"])

        n_wf = P.op("vector", lambda e: e.tensor_copy(we_b[:], vtp[0:8, 0:512]),
                    waits=[(S["rr"], n_wt)], inc=S["rt"])

        n_pm = P.op("tensor", lambda e: e.matmul(
            kvp[1][0][0:32, :], limeb[:], we_b[:], start=True, stop=True),
            waits=[(S["rt"], n_wf)], inc=S["rr"])

        def f_v(e):
            zpv = zpbf[:].rearrange("p (a two) -> p two a", two=2)
            zfv = zf[:].rearrange("p (a two) -> p two a", two=2)
            e.tensor_tensor(zpv[:, 0, :], zfv[:, 0, :], kvp[1][0][0:32, :], ALU.mult)
            return e.tensor_tensor(zpv[:, 1, :], zfv[:, 1, :], kvp[1][0][0:32, :],
                                   ALU.mult)
        n_zp = P.op("vector", f_v, waits=[(S["rr"], n_pm)], inc=S["rt"])

        if phases == 2:
            dumps = [
                (out_d[0:32, :], zpbf[:]), (out_d[32:64, :], zf[:]),
                (out_d[64:192, 0:32], lsb[:]), (out_d[192:320, 0:32], probs[:]),
                (out_d[320:448, 0:32], mskv[:]), (out_d[448:576, 0:32], rcol[:]),
                (out_d[576:584, 0:512], hslf[:]), (out_d[584:592, 0:512], dslf[:]),
                (out_d[592:593, 0:4], mxg[:]), (out_d[593:594, 0:16], tiny[:]),
                (out_d[728:856, 0:32], wnb[:]),
                (out_d[860:988, 0:32], expv[:]),
                (out_d[990:998, 0:32], limeb[:]),
            ]
            first = True
            for dd, ss_ in dumps:
                w = [(S["rt"], n_zp)] if first else []
                first = False
                P.op("gpsimd", lambda e, d=dd, s2=ss_: e.dma_start(out=d, in_=s2),
                     waits=w, inc=S["o4"], inck=16)
            P.wait("gpsimd", S["o4"], 16 * 13)
            return _finish(nc, P, blk, st)
        P.barrier_inc("vector", S["b23"])
        P.barrier_inc("tensor", S["b23"])
        P.barrier_inc("scalar", S["b23"])
        st1.close()

        st3 = st.enter_context(ExitStack())
        sb3 = lambda *a: st3.enter_context(nc.sbuf_tensor(*a))
        ksb = sb3("ksb", [128, 2 * 4096], BF16)
        vtsb = sb3("vtsb", [128, 2 * 4096], BF16)
        Pt = [sb3(f"Pt{i}", [128, TL], BF16) for i in range(2)]
        qring = [sb3(f"qring{i}", [128, TL], BF16) for i in range(4)]
        ndl = sb3("ndl", [1, TL], F32)
        dnt = sb3("dnt", [1, TL], F32)
        rbn = sb3("rbn", [128, TL], F32)
        ps3 = lambda n: st3.enter_context(nc.psum_tensor(n, [128, 512], F32))
        sp = [[ps3(f"sp{i}{j}") for j in range(2)] for i in range(2)]
        dn = [ps3(f"dn{j}") for j in range(2)]
        op = [ps3(f"op{j}") for j in range(2)]

        ts_cnt = {}
        se_cnt = {}
        tv_cnt = {}
        sl_cnt = {}
        tb_cnt = {}
        sr_cnt = {}
        vo_cnt = {}
        ql_cnt = {}
        kq_cnt = {}

        def ktile(kvh2, j):
            return ksb[:, (kvh2 * 32 + j) * 128:(kvh2 * 32 + j + 1) * 128]

        def vttile(kvh2, j):
            return vtsb[:, (kvh2 * 32 + j) * 128:(kvh2 * 32 + j + 1) * 128]

        def emit_boundary(hp):
            def f_v(e):
                e.tensor_copy(dnt[0:1, 0:512], dn[0][0:1, :])
                e.tensor_copy(dnt[0:1, 512:1024], dn[1][0:1, :])
                return e.reciprocal(ndl[:], dnt[:])
            sl_cnt[hp] = P.op("vector", f_v, waits=[(S["tv"], tv_cnt[(hp, 31)])],
                              inc=S["sl"])

            def f_t(e):
                e.matmul(dn[0][:], onesf[:], ndl[0:1, 0:512], start=True, stop=True,
                         skip_group_check=True)
                return e.matmul(dn[1][:], onesf[:], ndl[0:1, 512:1024], start=True,
                                stop=True, skip_group_check=True)
            tb_cnt[hp] = P.op("tensor", f_t, waits=[(S["sl"], sl_cnt[hp])],
                              inc=S["tb"])

            def f_s(e):
                e.activation(rbn[:, 0:512], dn[0][:], AF.Copy)
                return e.activation(rbn[:, 512:1024], dn[1][:], AF.Copy)
            wrb = [(S["tb"], tb_cnt[hp])]
            if hp >= 1:
                wrb.append((S["vo"], vo_cnt[hp - 1]))
            sr_cnt[hp] = P.op("scalar", f_s, waits=wrb, inc=S["sr"])

            def f_v(e):
                e.tensor_tensor(osb[:, hp * TL:hp * TL + 512], op[0][:],
                                rbn[:, 0:512], ALU.mult)
                return e.tensor_tensor(osb[:, hp * TL + 512:(hp + 1) * TL],
                                       op[1][:], rbn[:, 512:1024], ALU.mult)
            vo_cnt[hp] = P.op("vector", f_v, waits=[(S["sr"], sr_cnt[hp])],
                              inc=S["vo"])

        for qt in range(4):
            for kvh2 in range(2):
                kvh = 2 * qt + kvh2
                if qt == 0:
                    w = [(S["cc"], n_ag), (S["b23"], 3)] if kvh2 == 0 else []
                else:
                    lh = (qt - 1) * 8 + kvh2 * 4 + 3
                    w = [(S["ts"], ts_cnt[(lh, 31)]), (S["tv"], tv_cnt[(lh, 31)])]
                for r in range(4):
                    dstk = ksb[:, kvh2 * 4096:(kvh2 + 1) * 4096].rearrange(
                        "p (i f m) -> p i f m", i=8, f=4)[:, :, r, :]
                    srck = kvdst[r * 128:(r + 1) * 128,
                                 kvh * TL:(kvh + 1) * TL].rearrange(
                        "q (i m) -> q i m", i=8)
                    P.op("sync", lambda e, d=dstk, s=srck: e.dma_start(out=d, in_=s),
                         waits=w, inc=S["kq"], inck=16)
                    w = []
                    dstv = vtsb[:, kvh2 * 4096:(kvh2 + 1) * 4096].rearrange(
                        "p (i f m) -> p i f m", i=8, f=4)[:, :, r, :]
                    srcv = kvdst[r * 128:(r + 1) * 128, 8192:16384].rearrange(
                        "q (i c m) -> q i c m", i=8, c=8)[:, :, kvh, :]
                    P.op("sync", lambda e, d=dstv, s=srcv: e.dma_start(out=d, in_=s),
                         waits=w, inc=S["kq"], inck=16)
                kq_cnt[(qt, kvh2)] = S["kq"].n

            for hh in range(8):
                h = qt * 8 + hh
                kvh2 = hh // 4

                w = []
                if h == 0:
                    w.append((S["b23"], 3))
                if h >= 4:
                    w.append((S["ts"], ts_cnt[(h - 4, 31)]))
                ql_cnt[h] = P.op("gpsimd", lambda e, h=h: e.dma_start(
                    out=qring[h % 4][:], in_=q_dram[h][:, :]),
                    waits=w, inc=S["ql"], inck=16)

                qr = qring[h % 4]

                def emit_tv(h, j, kvh2):
                    jp = j % 2
                    lo = 128 * (j // 4)
                    w = [(S["se"], se_cnt[(h, j)])]
                    if j == 0 and h >= 1:
                        w.append((S["vo"], vo_cnt[h - 1]))

                    def f_t(e, j=j, jp=jp, lo=lo, kvh2=kvh2):
                        st_ = (j == 0)
                        vt = vttile(kvh2, j)
                        if lo < 512:
                            e.matmul(dn[0][0:1, lo:512], onesb[:], Pt[jp][:, lo:512],
                                     start=st_, stop=(j == 31), skip_group_check=True)
                            e.matmul(dn[1][0:1, :], onesb[:], Pt[jp][:, 512:1024],
                                     start=st_, stop=(j == 31), skip_group_check=True)
                            e.matmul(op[0][:, lo:512], vt, Pt[jp][:, lo:512],
                                     start=st_, stop=(j == 31), skip_group_check=True)
                            return e.matmul(op[1][:], vt, Pt[jp][:, 512:1024],
                                            start=st_, stop=(j == 31),
                                            skip_group_check=True)
                        e.matmul(dn[1][0:1, lo - 512:512], onesb[:], Pt[jp][:, lo:1024],
                                 start=False, stop=(j == 31), skip_group_check=True)
                        return e.matmul(op[1][:, lo - 512:512], vt, Pt[jp][:, lo:1024],
                                        start=False, stop=(j == 31),
                                        skip_group_check=True)
                    tv_cnt[(h, j)] = P.op("tensor", f_t, waits=w, inc=S["tv"])

                for j in range(32):
                    jp = j % 2
                    lo = 128 * (j // 4)
                    w = []
                    if j == 0:
                        w.append((S["ql"], ql_cnt[h]))
                        if hh == 0:
                            w.append((S["kq"], kq_cnt[(qt, 0)]))
                        if hh == 4:
                            w.append((S["kq"], kq_cnt[(qt, 1)]))
                        if h >= 1:
                            w.append((S["se"], se_cnt[(h - 1, 30)]))
                    elif j == 1:
                        if h >= 1:
                            w.append((S["se"], se_cnt[(h - 1, 31)]))
                    else:
                        w.append((S["se"], se_cnt[(h, j - 2)]))

                    def f_t(e, kvh2=kvh2, j=j, jp=jp, lo=lo, qr=qr):
                        kt = ktile(kvh2, j)
                        if lo < 512:
                            e.matmul(sp[jp][0][:, lo:512], kt, qr[:, lo:512],
                                     start=True, stop=True, skip_group_check=True)
                            e.matmul(sp[jp][1][:], kt, qr[:, 512:1024],
                                     start=True, stop=True, skip_group_check=True)
                            mt = sp[jp][0][:, lo:lo + 128]
                        else:
                            e.matmul(sp[jp][1][:, lo - 512:512], kt, qr[:, lo:1024],
                                     start=True, stop=True, skip_group_check=True)
                            mt = sp[jp][1][:, lo - 512:lo - 512 + 128]
                        return e.matmul(mt, masktb[:, (j % 4) * 128:(j % 4 + 1) * 128],
                                        idb[:], start=False, stop=True,
                                        skip_group_check=True)
                    ts_cnt[(h, j)] = P.op("tensor", f_t, waits=w, inc=S["ts"])

                    w = [(S["ts"], ts_cnt[(h, j)])]
                    if j >= 2:
                        w.append((S["tv"], tv_cnt[(h, j - 2)]))
                    if h == 0 and j == 0:
                        w.append((S["rt"], n_bcv))

                    def f_s(e, jp=jp, lo=lo):
                        if lo < 512:
                            e.activation(Pt[jp][:, lo:512], sp[jp][0][:, lo:512],
                                         AF.Exp, bias=negb[:], scale=SCALE)
                            return e.activation(Pt[jp][:, 512:1024], sp[jp][1][:],
                                                AF.Exp, bias=negb[:], scale=SCALE)
                        return e.activation(Pt[jp][:, lo:1024],
                                            sp[jp][1][:, lo - 512:512],
                                            AF.Exp, bias=negb[:], scale=SCALE)
                    se_cnt[(h, j)] = P.op("scalar", f_s, waits=w, inc=S["se"])

                    if j == 0 and h >= 1:
                        emit_boundary(h - 1)

                    if j >= 1:
                        emit_tv(h, j - 1, kvh2)

                emit_tv(h, 31, kvh2)

        emit_boundary(31)

        if phases == 3:
            P.op("gpsimd", lambda e: e.dma_start(out=out_d[0:128, :],
                                                 in_=osb[:, 31 * TL:32 * TL]),
                 waits=[(S["vo"], vo_cnt[31])], inc=S["o4"], inck=16)
            P.wait("gpsimd", S["o4"], 16)
            return _finish(nc, P, blk, st)
        P.barrier_inc("vector", S["b34"])
        P.barrier_inc("tensor", S["b34"])
        P.barrier_inc("scalar", S["b34"])
        st3.close()

        st4 = st.enter_context(ExitStack())
        sb4 = lambda *a: st4.enter_context(nc.sbuf_tensor(*a))
        wf32e = sb4("wf32e", [128, D], F32)
        wobf = sb4("wobf", [128, D], BF16)
        wbfe = sb4("wbfe", [128, D], BF16)
        xoc = [sb4(f"xoc{i}", [128, TL], F32) for i in range(2)]
        outsb = [sb4(f"outsb{i}", [128, TL], F32) for i in range(2)]
        gsb = [sb4(f"gsb{i}", [128, TL], F32) for i in range(2)]
        tmp4 = sb4("tmp4", [128, TL], F32)
        ps4 = lambda n: st4.enter_context(nc.psum_tensor(n, [128, 512], F32))
        sP = [[ps4(f"sP{i}{j}") for j in range(2)] for i in range(2)]
        fP = [[ps4(f"fP{i}{j}") for j in range(2)] for i in range(2)]

        t4s = {}
        t4f = {}
        s4g = {}
        v4c = {}
        wc4 = {}
        x4c = {}

        for oc in range(DC):
            p = oc % 2
            w = [(S["b34"], 3)] if oc == 0 else [(S["wc4"], 2 * oc)]
            P.op("sync", lambda e, oc=oc: e.dma_start(out=wf32e[:], in_=wo_d[oc][:, :]),
                 waits=w, inc=S["wl4"], inck=16)
            P.op("sync", lambda e, oc=oc: e.dma_start(out=wf32e[:], in_=w_d[oc][:, :]),
                 waits=[(S["wc4"], 2 * oc + 1)], inc=S["wl4"], inck=16)

            w = [(S["wl4"], 16 * (2 * oc + 1))]
            if oc >= 1:
                w.append((S["t4"], t4s[oc - 1]))
            wc4[2 * oc] = P.op("scalar", lambda e: e.activation(wobf[:], wf32e[:],
                                                                AF.Copy),
                               waits=w, inc=S["wc4"])
            w = [(S["wl4"], 16 * (2 * oc + 2))]
            if oc >= 1:
                w.append((S["t4"], t4f[oc - 1]))
            wc4[2 * oc + 1] = P.op("scalar", lambda e: e.activation(wbfe[:], wf32e[:],
                                                                    AF.Copy),
                                   waits=w, inc=S["wc4"])

            w = []
            if oc == 0:
                w.append((S["b34"], 3))
            if oc >= 2:
                w.append((S["v4"], v4c[oc - 2]))
            x4c[oc] = P.op("gpsimd", lambda e, p=p, oc=oc: e.dma_start(
                out=xoc[p][:], in_=x_d[oc * 128:(oc + 1) * 128, :]),
                waits=w, inc=S["x4"], inck=16)

            w = [(S["wc4"], wc4[2 * oc])]
            if oc == 0:
                w.append((S["vo"], vo_cnt[31]))
            if oc >= 2:
                w.append((S["v4"], v4c[oc - 2]))

            def f_t(e, p=p):
                for hc in range(H):
                    st_, sp_ = (hc == 0), (hc == H - 1)
                    lw = wobf[:, hc * 128:(hc + 1) * 128]
                    e.matmul(sP[p][0][:], lw, osb[:, hc * TL:hc * TL + 512],
                             start=st_, stop=sp_)
                    ins = e.matmul(sP[p][1][:], lw, osb[:, hc * TL + 512:(hc + 1) * TL],
                                   start=st_, stop=sp_)
                return ins
            t4s[oc] = P.op("tensor", f_t, waits=w, inc=S["t4"])

            w = [(S["wc4"], wc4[2 * oc + 1])]
            if oc == 0:
                w.append((S["rt"], n_zp))
            if oc >= 2:
                w.append((S["s4"], s4g[oc - 2]))

            def f_t(e, p=p, oc=oc):
                for dc in range(DC):
                    st_ = (dc == 0)
                    lw = wbfe[:, dc * 128:(dc + 1) * 128]
                    e.matmul(fP[p][0][:], lw, xn[:, dc * TL:dc * TL + 512],
                             start=st_, stop=False, skip_group_check=True)
                    e.matmul(fP[p][1][:], lw, xn[:, dc * TL + 512:(dc + 1) * TL],
                             start=st_, stop=False, skip_group_check=True)
                lb = bmb[:, oc * 128:(oc + 1) * 128]
                e.matmul(fP[p][0][:], lb, zpbf[:, 0:512], start=False, stop=True,
                         skip_group_check=True)
                return e.matmul(fP[p][1][:], lb, zpbf[:, 512:1024], start=False,
                                stop=True, skip_group_check=True)
            t4f[oc] = P.op("tensor", f_t, waits=w, inc=S["t4"])

            def f_s(e, p=p, oc=oc):
                e.activation(gsb[p][:, 0:512], fP[p][0][:], AF.Silu,
                             bias=bsb[:, oc:oc + 1], scale=1.0)
                return e.activation(gsb[p][:, 512:1024], fP[p][1][:], AF.Silu,
                                    bias=bsb[:, oc:oc + 1], scale=1.0)
            wsl = [(S["t4"], t4f[oc])]
            if oc >= 2:
                wsl.append((S["v4"], v4c[oc - 2]))
            s4g[oc] = P.op("scalar", f_s, waits=wsl, inc=S["s4"])

            w = [(S["s4"], s4g[oc]), (S["t4"], t4s[oc]), (S["x4"], x4c[oc])]
            if oc >= 2:
                w.append((S["o4"], 16 * (oc - 1)))

            def f_v(e, p=p):
                e.tensor_tensor(tmp4[:, 0:512], sP[p][0][:], gsb[p][:, 0:512], ALU.mult)
                e.tensor_tensor(tmp4[:, 512:1024], sP[p][1][:], gsb[p][:, 512:1024],
                                ALU.mult)
                e.tensor_tensor(outsb[p][:, 0:512], tmp4[:, 0:512], xoc[p][:, 0:512],
                                ALU.add)
                return e.tensor_tensor(outsb[p][:, 512:1024], tmp4[:, 512:1024],
                                       xoc[p][:, 512:1024], ALU.add)
            v4c[oc] = P.op("vector", f_v, waits=w, inc=S["v4"])

            P.op("gpsimd", lambda e, p=p, oc=oc: e.dma_start(
                out=out_d[oc * 128:(oc + 1) * 128, :], in_=outsb[p][:]),
                waits=[(S["v4"], v4c[oc])], inc=S["o4"], inck=16)

        P.wait("gpsimd", S["o4"], 16 * DC)

        _emit_all(P, blk)
        st4.close()
    return nc


def _emit_all(P, blk):
    def emit(eng_name):
        def body(e):
            for fn in P.q[eng_name]:
                fn(e)
        return body
    blk.sync(emit("sync"))
    blk.vector(emit("vector"))
    blk.scalar(emit("scalar"))
    blk.tensor(emit("tensor"))
    blk.gpsimd(emit("gpsimd"))


def _finish(nc, P, blk, st):
    _emit_all(P, blk)
    return nc



def _llama3_inv_freq():
    base = 500000.0
    inv_freq = 1.0 / base ** (np.arange(0, HD, 2, dtype=np.float32) / HD)
    factor, low_f, high_f, old_ctx = 32.0, 1.0, 4.0, 8192
    low_wl, high_wl = old_ctx / low_f, old_ctx / high_f
    wavelen = 2 * math.pi / inv_freq
    inv_llama = np.where(wavelen > low_wl, inv_freq / factor, inv_freq)
    smooth = (old_ctx / wavelen - low_f) / (high_f - low_f)
    smoothed = (1 - smooth) * inv_llama / factor + smooth * inv_llama
    is_med = ~(wavelen < high_wl) & ~(wavelen > low_wl)
    return np.where(is_med, smoothed, inv_llama).astype(np.float32)


_NC_CACHE = {}


def _get_nc():
    import os
    ph = int(os.environ.get("MK_PHASES", "4"))
    key = ("nc", ph)
    if key not in _NC_CACHE:
        _NC_CACHE[key] = build_program(ph)
    return _NC_CACHE[key]


def kernel(L0_attn_x, L0_attn_Wq, L0_attn_Wk, L0_attn_Wv, L0_attn_Wo,
           L0_ffn_W, L0_ffn_b, L0_ffn_A, L0_ffn_Bmat, L0_ffn_LiMEs):
    x = np.ascontiguousarray(np.asarray(L0_attn_x, dtype=np.float32))
    Wq = np.asarray(L0_attn_Wq, dtype=np.float32)
    Wk = np.asarray(L0_attn_Wk, dtype=np.float32)
    Wv = np.asarray(L0_attn_Wv, dtype=np.float32)
    Wo = np.asarray(L0_attn_Wo, dtype=np.float32)
    W = np.asarray(L0_ffn_W, dtype=np.float32)
    bvec = np.asarray(L0_ffn_b, dtype=np.float32)
    A = np.asarray(L0_ffn_A, dtype=np.float32)
    Bmat = np.asarray(L0_ffn_Bmat, dtype=np.float32)
    LiMEs = np.asarray(L0_ffn_LiMEs, dtype=np.float32)

    def stage_blocks(Wm, nblk):
        return np.ascontiguousarray(
            Wm.reshape(nblk, 128, DC, 128).transpose(0, 3, 2, 1).reshape(nblk, 128, D))

    wq_s = stage_blocks(Wq, H)
    wk_s = stage_blocks(Wk, KVH)
    wv_s = stage_blocks(Wv, KVH)
    wo_s = stage_blocks(Wo, DC)
    w_s = stage_blocks(W, DC)
    w8_s = np.ascontiguousarray(
        W[0:8].reshape(8, DC, 128).transpose(2, 1, 0).reshape(128, DC * 8))
    a_s = np.ascontiguousarray(
        A.reshape(DC, 128, 32).transpose(1, 0, 2).reshape(128, DC * 32))
    limt_s = np.ascontiguousarray(LiMEs.T)
    b_s = np.ascontiguousarray(bvec.reshape(DC, 128).T)
    b8_s = np.ascontiguousarray(bvec[0:8].reshape(8, 1))

    idb = np.eye(128, dtype=ml_dtypes.bfloat16)
    idf = np.eye(8, dtype=np.float32)
    onesb = np.ones((128, 1), dtype=ml_dtypes.bfloat16)
    onesf = np.ones((1, 128), dtype=np.float32)
    perm = np.zeros((128, 128), dtype=np.float32)
    for m in range(64):
        perm[m + 64, m] = -1.0
        perm[m, m + 64] = 1.0
    perm = perm.astype(ml_dtypes.bfloat16)

    inv = _llama3_inv_freq()

    in_maps = []
    token_ids = {}
    for c in range(NCORE):
        b, r = c // 4, c % 4
        tid = np.concatenate([np.arange(128 * (4 * i + r), 128 * (4 * i + r) + 128)
                              for i in range(8)])
        token_ids[c] = (b, tid)
        xs = np.ascontiguousarray(x[b][tid].T)

        pos = tid.astype(np.float32)
        ang = pos[None, :] * inv[:, None]
        cosl = np.ascontiguousarray(np.concatenate([np.cos(ang), np.cos(ang)], axis=0)
                                    .astype(ml_dtypes.bfloat16))
        sinl = np.ascontiguousarray(np.concatenate([np.sin(ang), np.sin(ang)], axis=0)
                                    .astype(ml_dtypes.bfloat16))

        mt = np.zeros((128, 512), dtype=np.float32)
        qt_i = np.arange(128)[:, None]
        kt_i = np.arange(128)[None, :]
        for jm in range(4):
            tile = mt[:, jm * 128:(jm + 1) * 128]
            if jm == r:
                tile[kt_i > qt_i] = NEGBIG
            elif jm > r:
                tile[:, :] = NEGBIG
        mtb = mt.astype(ml_dtypes.bfloat16)

        in_maps.append({
            "x": xs, "wq": wq_s, "wk": wk_s, "wv": wv_s, "wo": wo_s, "wff": w_s,
            "w8": w8_s, "amat": a_s, "bmat": Bmat, "lime": LiMEs, "limt": limt_s,
            "bvec": b_s, "b8": b8_s, "cosl": cosl, "sinl": sinl, "permb": perm,
            "identb": idb, "identf": idf, "masktb": mtb, "onesb": onesb,
            "onesf": onesf,
        })

    nc = _get_nc()
    res = run_bass_kernel_spmd(nc, in_maps, core_ids=list(range(NCORE)))

    out = np.empty((B, T, D), dtype=np.float32)
    for c in range(NCORE):
        b, tid = token_ids[c]
        out[b, tid, :] = res.results[c]["out"].T
    return out


if __name__ == "__main__":
    import sys
    if len(sys.argv) > 1 and sys.argv[1] == "build":
        nc = build_program()
        n_inst = sum(1 for _ in nc.all_instructions())
        print(f"built ok: {n_inst} instructions")


# version 11 (final)
# speedup vs baseline: 1.0474x; 1.0474x over previous
import math
from contextlib import ExitStack

import numpy as np
import ml_dtypes

import concourse.bass as bass
import concourse.mybir as mybir
from concourse.bass_utils import run_bass_kernel_spmd

F32 = mybir.dt.float32
BF16 = mybir.dt.bfloat16
AF = mybir.ActivationFunctionType
ALU = mybir.AluOpType
AX = mybir.AxisListType

B, T, D = 2, 4096, 4096
H, KVH, HD = 32, 8, 128
DC = D // 128
TL = 1024
NCORE = 8
SCALE = 1.0 / math.sqrt(HD)
NEGBIG = -30000.0
ENGS = ("sync", "vector", "scalar", "tensor", "gpsimd")


class Sem:

    def __init__(self, st, nc, name):
        self.h = st.enter_context(nc.semaphore(name))
        self.n = 0


class Plan:

    def __init__(self):
        self.q = {e: [] for e in ENGS}

    def op(self, eng, fn, waits=(), inc=None, inck=1):
        waits = [(s.h, v) for (s, v) in waits]
        semh = inc.h if inc is not None else None

        def run(e):
            for h, v in waits:
                e.wait_ge(h, v)
            ins = fn(e)
            if semh is not None:
                ins.then_inc(semh, inck)
        self.q[eng].append(run)
        if inc is not None:
            inc.n += inck
        return inc.n if inc is not None else None

    def wait(self, eng, sem, v):
        self.q[eng].append(lambda e, h=sem.h, v=v: e.wait_ge(h, v))

    def barrier_inc(self, eng, sem):
        self.q[eng].append(lambda e, h=sem.h: e.nop().then_inc(h, 1))
        sem.n += 1


def build_program(phases=4):
    nc = bass.Bass("TRN2", target_bir_lowering=False)

    x_d = nc.dram_tensor("x", [D, TL], F32, kind="ExternalInput").ap()
    wq_d = nc.dram_tensor("wq", [H, 128, D], F32, kind="ExternalInput").ap()
    wk_d = nc.dram_tensor("wk", [KVH, 128, D], F32, kind="ExternalInput").ap()
    wv_d = nc.dram_tensor("wv", [KVH, 128, D], F32, kind="ExternalInput").ap()
    wo_d = nc.dram_tensor("wo", [DC, 128, D], F32, kind="ExternalInput").ap()
    w_d = nc.dram_tensor("wff", [DC, 128, D], F32, kind="ExternalInput").ap()
    w8_d = nc.dram_tensor("w8", [128, DC * 8], F32, kind="ExternalInput").ap()
    a_d = nc.dram_tensor("amat", [128, DC * 32], F32, kind="ExternalInput").ap()
    bmat_d = nc.dram_tensor("bmat", [32, D], F32, kind="ExternalInput").ap()
    lime_d = nc.dram_tensor("lime", [8, 32], F32, kind="ExternalInput").ap()
    limt_d = nc.dram_tensor("limt", [32, 8], F32, kind="ExternalInput").ap()
    b_d = nc.dram_tensor("bvec", [128, DC], F32, kind="ExternalInput").ap()
    b8_d = nc.dram_tensor("b8", [8, 1], F32, kind="ExternalInput").ap()
    cos_d = nc.dram_tensor("cosl", [128, TL], BF16, kind="ExternalInput").ap()
    sin_d = nc.dram_tensor("sinl", [128, TL], BF16, kind="ExternalInput").ap()
    perm_d = nc.dram_tensor("permb", [128, 128], BF16, kind="ExternalInput").ap()
    idb_d = nc.dram_tensor("identb", [128, 128], BF16, kind="ExternalInput").ap()
    idf_d = nc.dram_tensor("identf", [8, 8], F32, kind="ExternalInput").ap()
    mask_d = nc.dram_tensor("masktb", [128, 512], BF16, kind="ExternalInput").ap()
    onesb_d = nc.dram_tensor("onesb", [128, 1], BF16, kind="ExternalInput").ap()
    onesf_d = nc.dram_tensor("onesf", [1, 128], F32, kind="ExternalInput").ap()
    out_d = nc.dram_tensor("out", [D, TL], F32, kind="ExternalOutput").ap()

    kvsrc = nc.dram_tensor("kvsrc", [128, 16384], BF16, kind="Internal").ap()
    kvdst = nc.dram_tensor("kvdst", [512, 16384], BF16, kind="Internal").ap()
    mxsrc = nc.dram_tensor("mxsrc", [1, 4], F32, kind="Internal").ap()
    mxdst = nc.dram_tensor("mxdst", [1, 4], F32, kind="Internal").ap()
    q_dram = nc.dram_tensor("q_dram", [H, 128, TL], BF16, kind="Internal").ap()

    G4 = [[0, 1, 2, 3], [4, 5, 6, 7]]
    G8 = [[0, 1, 2, 3, 4, 5, 6, 7]]

    P = Plan()

    with ExitStack() as st:
        sbt = lambda *a: st.enter_context(nc.sbuf_tensor(*a))
        xn = sbt("xn", [128, DC * TL], BF16)
        osb = sbt("osb", [128, H * TL], BF16)
        idb = sbt("idb", [128, 128], BF16)
        idf = sbt("idf", [8, 8], F32)
        onesb = sbt("onesb_s", [128, 1], BF16)
        onesf = sbt("onesf_s", [1, 128], F32)
        masktb = sbt("masktb_s", [128, 512], BF16)
        bsb = sbt("bsb", [128, DC], F32)
        zpbf = sbt("zpbf", [32, TL], BF16)
        zf = sbt("zf", [32, TL], F32)
        bmb = sbt("bmb", [32, D], BF16)
        negb = sbt("negb", [128, 1], F32)
        mx4 = sbt("mx4", [1, 4], F32)
        mxg = sbt("mxg", [1, 4], F32)

        S = {}
        for name in ("c", "xl", "v0", "t0", "s0", "b01", "wl", "wc", "kv", "r1",
                     "e1", "g1", "tp", "cc", "q2", "v2", "s2", "t2", "gq", "rt",
                     "rs", "rr", "ar", "b23", "kq", "ql", "ts", "se", "tv",
                     "sl", "tb", "sr", "vo", "b34", "wl4", "wc4", "x4", "t4",
                     "s4", "v4", "o4"):
            S[name] = Sem(st, nc, "s_" + name)

        blk = st.enter_context(nc.Block())

        st0 = st.enter_context(ExitStack())
        sb0 = lambda *a: st0.enter_context(nc.sbuf_tensor(*a))
        xf = [sb0(f"xf{i}", [128, TL], F32) for i in range(2)]
        x2f = sb0("x2f", [128, TL], F32)
        xb2 = [sb0(f"xb2{i}", [128, TL], BF16) for i in range(2)]
        x2b = [sb0(f"x2b{i}", [128, TL], BF16) for i in range(2)]
        srow = sb0("srow", [1, 5 * TL], F32)
        ps0 = lambda n: st0.enter_context(nc.psum_tensor(n, [128, 512], F32))
        msum = [ps0(f"msum{i}") for i in range(2)]
        esum = [ps0(f"esum{i}") for i in range(2)]
        mbc = [ps0(f"mbc{i}") for i in range(2)]
        rbc = [ps0(f"rbc{i}") for i in range(2)]
        NEGM, MSQ, VAR, IVAR, RSTD = (srow[0:1, i * TL:(i + 1) * TL] for i in range(5))

        for src, dst in [(idb_d, idb), (idf_d, idf), (onesb_d, onesb),
                         (onesf_d, onesf), (mask_d, masktb), (b_d, bsb)]:
            P.op("sync", lambda e, s=src, d=dst: e.dma_start(out=d[:], in_=s[:, :]),
                 inc=S["c"], inck=16)
        NCONST0 = S["c"].n

        pass1v = {}
        stats_t = {}
        for dc in range(DC):
            p = dc % 2
            w = [(S["v0"], pass1v[dc - 2])] if dc >= 2 else []
            P.op("sync", lambda e, p=p, dc=dc: e.dma_start(
                out=xf[p][:], in_=x_d[dc * 128:(dc + 1) * 128, :]),
                waits=w, inc=S["xl"], inck=16)

            w = [(S["xl"], 16 * (dc + 1))]
            if dc == 0:
                w.append((S["c"], NCONST0))
            if dc >= 2:
                w.append((S["t0"], stats_t[dc - 2]))

            def f_v(e, p=p):
                e.tensor_tensor(x2f[:], xf[p][:], xf[p][:], ALU.mult)
                e.tensor_copy(xb2[p][:], xf[p][:])
                return e.tensor_copy(x2b[p][:], x2f[:])
            pass1v[dc] = P.op("vector", f_v, waits=w, inc=S["v0"])

            def f_t(e, p=p, dc=dc):
                st_, sp_ = (dc == 0), (dc == DC - 1)
                e.matmul(msum[0][0:1, :], onesb[:], xb2[p][:, 0:512], start=st_, stop=sp_)
                e.matmul(msum[1][0:1, :], onesb[:], xb2[p][:, 512:1024], start=st_, stop=sp_)
                e.matmul(esum[0][0:1, :], onesb[:], x2b[p][:, 0:512], start=st_, stop=sp_)
                return e.matmul(esum[1][0:1, :], onesb[:], x2b[p][:, 512:1024],
                                start=st_, stop=sp_)
            stats_t[dc] = P.op("tensor", f_t, waits=[(S["v0"], pass1v[dc])], inc=S["t0"])

        def f(e):
            e.tensor_scalar(NEGM[:, 0:512], msum[0][0:1, :], -1.0 / D, None, ALU.mult)
            return e.tensor_scalar(NEGM[:, 512:1024], msum[1][0:1, :], -1.0 / D,
                                   None, ALU.mult)
        n_negm = P.op("vector", f, waits=[(S["t0"], stats_t[DC - 1])], inc=S["v0"])

        n_msq = P.op("scalar", lambda e: e.activation(MSQ[:], NEGM[:], AF.Square),
                     waits=[(S["v0"], n_negm)], inc=S["s0"])

        def f(e):
            e.scalar_tensor_tensor(VAR[:, 0:512], esum[0][0:1, :], 1.0 / D,
                                   MSQ[:, 0:512], ALU.mult, ALU.subtract)
            e.scalar_tensor_tensor(VAR[:, 512:1024], esum[1][0:1, :], 1.0 / D,
                                   MSQ[:, 512:1024], ALU.mult, ALU.subtract)
            e.tensor_scalar(VAR[:], VAR[:], 1e-5, None, ALU.add)
            return e.reciprocal(IVAR[:], VAR[:])
        n_ivar = P.op("vector", f, waits=[(S["s0"], n_msq)], inc=S["v0"])

        n_rstd = P.op("scalar", lambda e: e.activation(RSTD[:], IVAR[:], AF.Sqrt),
                      waits=[(S["v0"], n_ivar)], inc=S["s0"])

        def f(e):
            e.matmul(mbc[0][:], onesf[:], NEGM[:, 0:512], start=True, stop=True)
            e.matmul(mbc[1][:], onesf[:], NEGM[:, 512:1024], start=True, stop=True)
            e.matmul(rbc[0][:], onesf[:], RSTD[:, 0:512], start=True, stop=True)
            return e.matmul(rbc[1][:], onesf[:], RSTD[:, 512:1024], start=True, stop=True)
        n_bc0 = P.op("tensor", f, waits=[(S["s0"], n_rstd)], inc=S["t0"])

        xn_v = {}
        for dc in range(DC):
            p = dc % 2
            if dc < 2:
                w = [(S["v0"], n_ivar)]
            else:
                w = [(S["v0"], xn_v[dc - 2])]
            P.op("sync", lambda e, p=p, dc=dc: e.dma_start(
                out=xf[p][:], in_=x_d[dc * 128:(dc + 1) * 128, :]),
                waits=w, inc=S["xl"], inck=16)

            w = [(S["xl"], 16 * (DC + dc + 1))]
            if dc == 0:
                w.append((S["t0"], n_bc0))

            def f_v(e, p=p, dc=dc):
                for hh in range(2):
                    c0, c1 = hh * 512, (hh + 1) * 512
                    e.tensor_tensor(x2f[:, c0:c1], xf[p][:, c0:c1], mbc[hh][:], ALU.add)
                    ins = e.tensor_tensor(xn[:, dc * TL + c0:dc * TL + c1],
                                          x2f[:, c0:c1], rbc[hh][:], ALU.mult)
                return ins
            xn_v[dc] = P.op("vector", f_v, waits=w, inc=S["v0"])
        XN_DONE = S["v0"].n

        if phases <= 0:
            P.op("gpsimd", lambda e: e.dma_start(out=out_d[0:128, :], in_=xn[:, 0:TL]),
                 waits=[(S["v0"], XN_DONE)], inc=S["o4"], inck=16)
            P.wait("gpsimd", S["o4"], 16)
            return _finish(nc, P, blk, st)
        P.barrier_inc("vector", S["b01"])
        P.barrier_inc("tensor", S["b01"])
        P.barrier_inc("scalar", S["b01"])
        st0.close()

        st1 = st.enter_context(ExitStack())
        sb1 = lambda *a: st1.enter_context(nc.sbuf_tensor(*a))
        cosl = sb1("cosl_s", [128, TL], BF16)
        sinl = sb1("sinl_s", [128, TL], BF16)
        permb = sb1("permb_s", [128, 128], BF16)
        wf32 = sb1("wf32", [128, D], F32)
        wbf = [sb1(f"wbf{i}", [128, D], BF16) for i in range(2)]
        qs_b = sb1("qs_b", [128, TL], BF16)
        qc_b = sb1("qc_b", [128, TL], BF16)
        ktmp = [sb1(f"ktmp{i}", [128, TL], BF16) for i in range(2)]
        vtmp0 = sb1("vtmp0", [128, TL], BF16)
        vtmp = [vtmp0, vtmp0]
        vttmp0 = sb1("vttmp0", [128, TL], BF16)
        vttmp = [vttmp0, vttmp0]
        sqb = sb1("sqb", [128, TL], BF16)
        qsb = [sb1(f"qsb{i}", [128, TL], BF16) for i in range(2)]
        abf = sb1("abf", [128, DC * 32], BF16)
        w8b = sb1("w8b", [128, DC * 8], BF16)
        limeb = sb1("limeb", [8, 32], BF16)
        limtb = sb1("limtb", [32, 8], BF16)
        limef = sb1("limef", [8, 32], F32)
        limtf = sb1("limtf", [32, 8], F32)
        b8f = sb1("b8f", [8, 1], F32)
        hslf = sb1("hslf", [8, 512], F32)
        dslf = sb1("dslf", [8, 512], F32)
        rt8 = sb1("rt8", [8, 2], F32)
        lsb = sb1("lsb", [128, 32], F32)
        expv = sb1("expv", [128, 32], F32)
        probs = sb1("probs", [128, 32], F32)
        mskv = sb1("mskv", [128, 32], F32)
        wnb = sb1("wnb", [128, 32], BF16)
        rcol = sb1("rcol", [128, 32], F32)
        tiny = sb1("tiny", [1, 16], F32)
        we_b = sb1("we_b", [8, 512], BF16)
        knm = sb1("knm", [1, 16], F32)
        qnm = sb1("qnm", [1, 64], F32)
        rHbc = sb1("rHbc", [128, 1], F32)
        rDbc = sb1("rDbc", [128, 1], F32)
        ps1 = lambda n, dt=F32: st1.enter_context(nc.psum_tensor(n, [128, 512], dt))
        kvp = [[ps1(f"kvp{i}{j}") for j in range(2)] for i in range(2)]
        rop = [ps1(f"rop{j}") for j in range(2)]
        vtp = st1.enter_context(nc.psum_tensor("vtp", [128, 1024], BF16))

        first = True
        for src, dst in [(cos_d, cosl), (sin_d, sinl), (perm_d, permb),
                         (lime_d, limef), (limt_d, limtf), (b8_d, b8f)]:
            w = [(S["b01"], 3)] if first else []
            first = False
            P.op("sync", lambda e, s=src, d=dst: e.dma_start(out=d[:], in_=s[:, :]),
                 waits=w, inc=S["c"], inck=16)

        def f(e):
            e.tensor_copy(limeb[:], limef[:])
            e.memset(mx4[:], 0.0)
            return e.tensor_copy(limtb[:], limtf[:])
        P.op("vector", f, waits=[(S["c"], S["c"].n)], inc=S["wc"])

        P.op("sync", lambda e: e.dma_start(out=wf32[0:32, :], in_=bmat_d[:, :]),
             waits=[(S["b01"], 3)], inc=S["c"], inck=16)
        P.op("vector", lambda e: e.tensor_scalar(bmb[:], wf32[0:32, :], 0.5,
                                                 None, ALU.mult),
             waits=[(S["c"], S["c"].n)], inc=S["wc"])
        P.op("sync", lambda e: e.dma_start(out=wf32[:, 0:DC * 32], in_=a_d[:, :]),
             waits=[(S["wc"], S["wc"].n)], inc=S["c"], inck=16)
        P.op("vector", lambda e: e.tensor_copy(abf[:], wf32[:, 0:DC * 32]),
             waits=[(S["c"], S["c"].n)], inc=S["wc"])
        P.op("sync", lambda e: e.dma_start(out=wf32[:, 0:DC * 8], in_=w8_d[:, :]),
             waits=[(S["wc"], S["wc"].n)], inc=S["c"], inck=16)
        P.op("vector", lambda e: e.tensor_copy(w8b[:], wf32[:, 0:DC * 8]),
             waits=[(S["c"], S["c"].n)], inc=S["wc"])
        CONST_CAST = S["wc"].n

        kv_done = {}
        kvp_free = {}
        perm_done = {}
        evk_done = {}
        sq_done = {}
        bnd_done = {}
        bndv_done = {}
        rv_done = {}
        ev_done = {}
        tpv_done = {}
        vte_done = {}
        g1_done = {}
        wc_done = {}

        def plan_wstream(n_global, dram_src, extra_sync_waits=(), consumer_free=None):
            w = list(extra_sync_waits)
            w.append((S["wc"], CONST_CAST + n_global))
            P.op("sync", lambda e, s=dram_src: e.dma_start(out=wf32[:], in_=s),
                 waits=w, inc=S["wl"], inck=16)
            w = [(S["wl"], 16 * (n_global + 1))]
            if consumer_free is not None:
                w.append(consumer_free)
            wc_done[n_global] = P.op(
                "scalar", lambda e, p=n_global % 2: e.activation(
                    wbf[p][:], wf32[:], AF.Copy),
                waits=w, inc=S["wc"])

        NKV = 16
        for n in range(NKV):
            p = n % 2
            src = (wk_d[n] if n < KVH else wv_d[n - KVH])[:, :]
            plan_wstream(n, src,
                         consumer_free=((S["kv"], kv_done[n - 2]) if n >= 2 else None))

            w = [(S["wc"], CONST_CAST + n + 1)]
            if n == 0:
                w.append((S["v0"], XN_DONE))
            if n >= 2:
                w.append(kvp_free[n - 2])

            def f_t(e, p=p):
                for dc in range(DC):
                    st_, sp_ = (dc == 0), (dc == DC - 1)
                    lw = wbf[p][:, dc * 128:(dc + 1) * 128]
                    e.matmul(kvp[p][0][:], lw, xn[:, dc * TL:dc * TL + 512],
                             start=st_, stop=sp_)
                    ins = e.matmul(kvp[p][1][:], lw, xn[:, dc * TL + 512:(dc + 1) * TL],
                                   start=st_, stop=sp_)
                return ins
            kv_done[n] = P.op("tensor", f_t, waits=w, inc=S["kv"])

            if n < KVH:
                w = [(S["kv"], kv_done[n])]
                if n >= 1:
                    w.append((S["tp"], perm_done[n - 1]))

                def f_v(e, p=p):
                    for hh in range(2):
                        c0, c1 = hh * 512, (hh + 1) * 512
                        e.tensor_tensor(qs_b[:, c0:c1], kvp[p][hh][:],
                                        sinl[:, c0:c1], ALU.mult)
                        ins = e.tensor_tensor(qc_b[:, c0:c1], kvp[p][hh][:],
                                              cosl[:, c0:c1], ALU.mult)
                    return ins
                rv_done[n] = P.op("vector", f_v, waits=w, inc=S["r1"])
                kvp_free[n] = (S["r1"], rv_done[n])

                w = [(S["r1"], rv_done[n])]
                if n >= 1:
                    w.append((S["r1"], bndv_done[n - 1]))

                def f_t(e):
                    for hh in range(2):
                        c0, c1 = hh * 512, (hh + 1) * 512
                        e.matmul(rop[hh][:], permb[:], qs_b[:, c0:c1],
                                 start=True, stop=False)
                        ins = e.matmul(rop[hh][:], idb[:], qc_b[:, c0:c1],
                                       start=False, stop=True)
                    return ins
                perm_done[n] = P.op("tensor", f_t, waits=w, inc=S["tp"])

                w = [(S["tp"], perm_done[n])]
                if n >= 2:
                    w.append((S["g1"], g1_done[n - 2]))

                def f_s(e, p=p):
                    e.activation(ktmp[p][:, 0:512], rop[0][:], AF.Copy)
                    return e.activation(ktmp[p][:, 512:1024], rop[1][:], AF.Copy)
                evk_done[n] = P.op("scalar", f_s, waits=w, inc=S["e1"])

                w = []
                if n >= 1:
                    w.append((S["tp"], bnd_done[n - 1]))
                sq_done[n] = P.op(
                    "scalar", lambda e, p=p: e.activation(sqb[:], ktmp[p][:], AF.Square),
                    waits=w, inc=S["e1"])

                def f_t(e):
                    e.matmul(rop[0][0:1, :], onesb[:], sqb[:, 0:512], start=True, stop=True)
                    return e.matmul(rop[1][0:1, :], onesb[:], sqb[:, 512:1024],
                                    start=True, stop=True)
                bnd_done[n] = P.op("tensor", f_t, waits=[(S["e1"], sq_done[n])],
                                   inc=S["tp"])

                def f_v(e, n=n):
                    e.reduce_max(knm[0:1, 2 * n:2 * n + 1], rop[0][0:1, :], axis=AX.X)
                    return e.reduce_max(knm[0:1, 2 * n + 1:2 * n + 2], rop[1][0:1, :],
                                        axis=AX.X)
                bndv_done[n] = P.op("vector", f_v, waits=[(S["tp"], bnd_done[n])],
                                    inc=S["r1"])

                g1_done[n] = P.op("gpsimd", lambda e, p=p, n=n: e.dma_start(
                    out=kvsrc[:, n * TL:(n + 1) * TL], in_=ktmp[p][:]),
                    waits=[(S["e1"], evk_done[n])], inc=S["g1"], inck=16)
            else:
                c = n - KVH
                w = [(S["kv"], kv_done[n])]
                if n - 1 >= KVH:
                    w.append((S["tp"], tpv_done[n - 1]))

                def f_s(e, p=p):
                    e.activation(vtmp[p][:, 0:512], kvp[p][0][:], AF.Copy)
                    return e.activation(vtmp[p][:, 512:1024], kvp[p][1][:], AF.Copy)
                ev_done[n] = P.op("scalar", f_s, waits=w, inc=S["e1"])
                kvp_free[n] = (S["e1"], ev_done[n])

                w = [(S["e1"], ev_done[n])]
                if n - 2 >= KVH:
                    w.append((S["r1"], vte_done[n - 2]))

                def f_t(e, p=p):
                    for g in range(8):
                        ins = e.transpose(vtp[:, g * 128:(g + 1) * 128],
                                          vtmp[p][:, g * 128:(g + 1) * 128], idb[:])
                    return ins
                tpv_done[n] = P.op("tensor", f_t, waits=w, inc=S["tp"])

                w = [(S["tp"], tpv_done[n])]
                if n - 1 >= KVH:
                    w.append((S["g1"], g1_done[n - 1]))
                vte_done[n] = P.op(
                    "vector", lambda e, p=p: e.tensor_copy(vttmp[p][:], vtp[:]),
                    waits=w, inc=S["r1"])

                dst = kvsrc[:, 8192:16384].rearrange(
                    "p (g c m) -> p g c m", g=8, c=8)[:, :, c, :]
                srcap = vttmp[p][:].rearrange("p (g m) -> p g m", g=8)
                g1_done[n] = P.op("gpsimd", lambda e, d=dst, s=srcap: e.dma_start(
                    out=d, in_=s), waits=[(S["r1"], vte_done[n])],
                    inc=S["g1"], inck=16)

        n_ag = P.op("gpsimd", lambda e: e.collective_compute(
            "AllGather", ALU.bypass, replica_groups=G4,
            ins=[kvsrc.opt()], outs=[kvdst.opt()]),
            waits=[(S["g1"], 16 * NKV)], inc=S["cc"])

        if phases == 1:
            P.op("gpsimd", lambda e: e.dma_start(out=out_d[0:128, :],
                                                 in_=ktmp[1][:]),
                 waits=[(S["cc"], n_ag)], inc=S["o4"], inck=16)
            P.wait("gpsimd", S["o4"], 16)
            return _finish(nc, P, blk, st)
        q_done = {}
        rv2 = {}
        perm2 = {}
        ev2 = {}
        sq2 = {}
        bnd2 = {}
        bndv2 = {}
        gq_done = {}
        for h in range(H):
            p = h % 2
            ng = NKV + h
            plan_wstream(ng, wq_d[h][:, :],
                         consumer_free=((S["q2"], q_done[h - 2]) if h >= 2
                                        else (S["kv"], kv_done[NKV - 2 + h])))

            w = [(S["wc"], CONST_CAST + ng + 1)]
            if h < 2:
                w.append(kvp_free[NKV - 2 + h])
            else:
                w.append((S["v2"], rv2[h - 2]))

            def f_t(e, p=p):
                for dc in range(DC):
                    st_, sp_ = (dc == 0), (dc == DC - 1)
                    lw = wbf[ (NKV + 0) % 2 ] if False else None
                    return None
            def f_t(e, p=p, ng=ng):
                wb = wbf[ng % 2]
                for dc in range(DC):
                    st_, sp_ = (dc == 0), (dc == DC - 1)
                    lw = wb[:, dc * 128:(dc + 1) * 128]
                    e.matmul(kvp[p][0][:], lw, xn[:, dc * TL:dc * TL + 512],
                             start=st_, stop=sp_)
                    ins = e.matmul(kvp[p][1][:], lw, xn[:, dc * TL + 512:(dc + 1) * TL],
                                   start=st_, stop=sp_)
                return ins
            q_done[h] = P.op("tensor", f_t, waits=w, inc=S["q2"])

            w = [(S["q2"], q_done[h])]
            if h == 0:
                w.append((S["tp"], perm_done[KVH - 1]))
            else:
                w.append((S["t2"], perm2[h - 1]))

            def f_v(e, p=p):
                for hh in range(2):
                    c0, c1 = hh * 512, (hh + 1) * 512
                    e.tensor_tensor(qs_b[:, c0:c1], kvp[p][hh][:], sinl[:, c0:c1], ALU.mult)
                    ins = e.tensor_tensor(qc_b[:, c0:c1], kvp[p][hh][:],
                                          cosl[:, c0:c1], ALU.mult)
                return ins
            rv2[h] = P.op("vector", f_v, waits=w, inc=S["v2"])

            w = [(S["v2"], rv2[h])]
            if h == 0:
                w.append((S["r1"], bndv_done[KVH - 1]))
            else:
                w.append((S["v2"], bndv2[h - 1]))

            def f_t(e):
                for hh in range(2):
                    c0, c1 = hh * 512, (hh + 1) * 512
                    e.matmul(rop[hh][:], permb[:], qs_b[:, c0:c1], start=True, stop=False)
                    ins = e.matmul(rop[hh][:], idb[:], qc_b[:, c0:c1],
                                   start=False, stop=True)
                return ins
            perm2[h] = P.op("tensor", f_t, waits=w, inc=S["t2"])

            w = [(S["t2"], perm2[h])]
            if h >= 2:
                w.append((S["gq"], gq_done[h - 2]))

            def f_s(e, p=p):
                e.activation(qsb[p][:, 0:512], rop[0][:], AF.Copy)
                return e.activation(qsb[p][:, 512:1024], rop[1][:], AF.Copy)
            ev2[h] = P.op("scalar", f_s, waits=w, inc=S["s2"])

            w = []
            if h == 0:
                w.append((S["tp"], bnd_done[KVH - 1]))
            else:
                w.append((S["t2"], bnd2[h - 1]))
            sq2[h] = P.op("scalar",
                          lambda e, p=p: e.activation(sqb[:], qsb[p][:], AF.Square),
                          waits=w, inc=S["s2"])

            w = [(S["s2"], sq2[h])]

            def f_t(e):
                e.matmul(rop[0][0:1, :], onesb[:], sqb[:, 0:512], start=True, stop=True)
                return e.matmul(rop[1][0:1, :], onesb[:], sqb[:, 512:1024],
                                start=True, stop=True)
            bnd2[h] = P.op("tensor", f_t, waits=w, inc=S["t2"])

            def f_v(e, h=h):
                e.reduce_max(qnm[0:1, 2 * h:2 * h + 1], rop[0][0:1, :], axis=AX.X)
                return e.reduce_max(qnm[0:1, 2 * h + 1:2 * h + 2], rop[1][0:1, :],
                                    axis=AX.X)
            bndv2[h] = P.op("vector", f_v, waits=[(S["t2"], bnd2[h])], inc=S["v2"])

            gq_done[h] = P.op("gpsimd", lambda e, p=p, h=h: e.dma_start(
                out=q_dram[h][:, :], in_=qsb[p][:]),
                waits=[(S["s2"], ev2[h])], inc=S["gq"], inck=16)

        w = [(S["v2"], rv2[31]), (S["wc"], CONST_CAST)]

        def f_t(e):
            for dc in range(DC):
                st_, sp_ = (dc == 0), (dc == DC - 1)
                la = abf[:, dc * 32:(dc + 1) * 32]
                e.matmul(kvp[1][0][0:32, :], la, xn[:, dc * TL:dc * TL + 512],
                         start=st_, stop=sp_)
                ins = e.matmul(kvp[1][1][0:32, :], la,
                               xn[:, dc * TL + 512:(dc + 1) * TL], start=st_, stop=sp_)
            return ins
        n_z = P.op("tensor", f_t, waits=w, inc=S["rr"])

        w = [(S["v2"], bndv2[31])]

        def f_t(e):
            for dc in range(DC):
                st_, sp_ = (dc == 0), (dc == DC - 1)
                lw = w8b[:, dc * 8:(dc + 1) * 8]
                e.matmul(rop[0][0:8, :], lw, xn[:, dc * TL:dc * TL + 512],
                         start=st_, stop=sp_)
                ins = e.matmul(rop[1][0:8, :], lw, xn[:, dc * TL + 512:(dc + 1) * TL],
                               start=st_, stop=sp_)
            return ins
        n_h8 = P.op("tensor", f_t, waits=w, inc=S["rr"])

        def f_v(e):
            e.tensor_copy(zf[:, 0:512], kvp[1][0][0:32, :])
            e.tensor_copy(zf[:, 512:1024], kvp[1][1][0:32, :])
            zv = zf[:].rearrange("p (a two) -> p two a", two=2)[:, 1, :]
            return e.tensor_copy(qs_b[0:32, 0:512], zv)
        n_zev = P.op("vector", f_v, waits=[(S["rr"], n_z), (S["t2"], perm2[31])],
                     inc=S["rt"])

        n_dsl = P.op("tensor", lambda e: e.matmul(
            kvp[1][0][0:8, :], limtb[:], qs_b[0:32, 0:512], start=True, stop=True),
            waits=[(S["rt"], n_zev)], inc=S["rr"])

        def f_v(e):
            h0 = rop[0][0:8, :].rearrange("p (a two) -> p two a", two=2)[:, 1, :]
            h1 = rop[1][0:8, :].rearrange("p (a two) -> p two a", two=2)[:, 1, :]
            e.tensor_scalar(hslf[:, 0:256], h0, b8f[:], None, ALU.add)
            return e.tensor_scalar(hslf[:, 256:512], h1, b8f[:], None, ALU.add)
        n_hsl = P.op("vector", f_v, waits=[(S["rr"], n_h8)], inc=S["rt"])

        n_dslf = P.op("vector", lambda e: e.tensor_copy(dslf[:], kvp[1][0][0:8, :]),
                      waits=[(S["rr"], n_dsl)], inc=S["rt"])

        def f_v(e):
            e.reduce_max(rt8[:, 0:1], hslf[:], axis=AX.X, apply_absolute_value=True)
            return e.reduce_max(rt8[:, 1:2], dslf[:], axis=AX.X,
                                apply_absolute_value=True)
        n_rt8 = P.op("vector", f_v, waits=[(S["rt"], n_dslf)], inc=S["rt"])

        def f_t(e):
            e.transpose(kvp[0][0][0:1, 0:8], rt8[0:8, 0:1], idf[0:8, 0:8])
            return e.transpose(kvp[0][0][0:1, 8:16], rt8[0:8, 1:2], idf[0:8, 0:8])
        n_tr8 = P.op("tensor", f_t, waits=[(S["rt"], n_rt8), (S["v2"], bndv2[30])],
                     inc=S["rr"])

        def f_v(e):
            e.reduce_max(mx4[0:1, 0:1], kvp[0][0][0:1, 0:8], axis=AX.X)
            e.reduce_max(mx4[0:1, 1:2], kvp[0][0][0:1, 8:16], axis=AX.X)
            e.reduce_max(mx4[0:1, 2:3], qnm[0:1, :], axis=AX.X)
            return e.reduce_max(mx4[0:1, 3:4], knm[0:1, :], axis=AX.X)
        n_mx = P.op("vector", f_v,
                    waits=[(S["rr"], n_tr8), (S["v2"], bndv2[31]),
                           (S["r1"], bndv_done[KVH - 1])], inc=S["rt"])

        P.op("gpsimd", lambda e: e.dma_start(out=mxsrc[:, :], in_=mx4[:]),
             waits=[(S["rt"], n_mx)], inc=S["ar"], inck=16)
        P.op("gpsimd", lambda e: e.collective_compute(
            "AllReduce", ALU.max, replica_groups=G8,
            ins=[mxsrc.opt()], outs=[mxdst.opt()]),
            waits=[(S["ar"], 16)], inc=S["ar"])
        P.op("gpsimd", lambda e: e.dma_start(out=mxg[:], in_=mxdst[:, :]),
             waits=[(S["ar"], 17)], inc=S["ar"], inck=16)
        AR_DONE = S["ar"].n

        def f_v(e):
            e.tensor_scalar_max(tiny[0:1, 0:1], mxg[0:1, 0:1], 1e-6)
            e.tensor_scalar_max(tiny[0:1, 1:2], mxg[0:1, 1:2], 1e-6)
            return e.tensor_tensor(tiny[0:1, 4:5], mxg[0:1, 2:3], mxg[0:1, 3:4], ALU.mult)
        n_sc0 = P.op("vector", f_v, waits=[(S["ar"], AR_DONE)], inc=S["rt"])

        def f_v(e):
            e.reciprocal(tiny[0:1, 8:9], tiny[0:1, 0:1])
            return e.reciprocal(tiny[0:1, 9:10], tiny[0:1, 1:2])
        n_sc1 = P.op("vector", f_v, waits=[(S["rt"], n_sc0)], inc=S["rt"])

        def f_v(e):
            e.tensor_scalar(tiny[0:1, 2:3], tiny[0:1, 8:9], 0.5, None, ALU.mult)
            return e.tensor_scalar(tiny[0:1, 3:4], tiny[0:1, 9:10], 0.5, None, ALU.mult)
        n_sc = P.op("vector", f_v, waits=[(S["rt"], n_sc1)], inc=S["rt"])

        n_sq = P.op("scalar", lambda e: e.activation(tiny[0:1, 5:6], tiny[0:1, 4:5],
                                                     AF.Sqrt),
                    waits=[(S["rt"], n_sc)], inc=S["rs"])
        n_nbA = P.op("vector", lambda e: e.tensor_scalar(
            tiny[0:1, 10:11], tiny[0:1, 5:6], -1.01 * SCALE, 75.0, ALU.mult, ALU.add),
            waits=[(S["rs"], n_sq)], inc=S["rt"])
        n_nb1 = P.op("vector", lambda e: e.tensor_scalar_min(
            tiny[0:1, 6:7], tiny[0:1, 10:11], 0.0),
            waits=[(S["rt"], n_nbA)], inc=S["rt"])

        def f_t(e):
            e.matmul(kvp[0][1][:, 0:1], onesf[:], tiny[0:1, 2:3], start=True, stop=True)
            e.matmul(kvp[0][1][:, 1:2], onesf[:], tiny[0:1, 3:4], start=True, stop=True)
            return e.matmul(kvp[0][1][:, 2:3], onesf[:], tiny[0:1, 6:7],
                            start=True, stop=True)
        n_bcr = P.op("tensor", f_t, waits=[(S["rt"], n_nb1)], inc=S["rr"])

        def f_v(e):
            e.tensor_copy(rHbc[:], kvp[0][1][:, 0:1])
            e.tensor_copy(rDbc[:], kvp[0][1][:, 1:2])
            return e.tensor_copy(negb[:], kvp[0][1][:, 2:3])
        n_bcv = P.op("vector", f_v, waits=[(S["rr"], n_bcr)], inc=S["rt"])

        def f_t(e):
            for t in range(4):
                e.transpose(rop[0][:, t * 8:(t + 1) * 8],
                            hslf[0:8, t * 128:(t + 1) * 128], idf[0:8, 0:8])
            for t in range(4):
                ins = e.transpose(rop[0][:, 32 + t * 8:32 + (t + 1) * 8],
                                  dslf[0:8, t * 128:(t + 1) * 128], idf[0:8, 0:8])
            return ins
        n_trl = P.op("tensor", f_t, waits=[(S["rt"], n_rt8)], inc=S["rr"])

        def f_v(e):
            e.wait_ge(S["rt"].h, n_bcv)
            e.tensor_scalar(expv[:], rop[0][:, 0:32], rHbc[:], None, ALU.mult)
            return e.scalar_tensor_tensor(lsb[:], rop[0][:, 32:64], rDbc[:],
                                          expv[:], ALU.mult, ALU.add)
        n_log = P.op("vector", f_v, waits=[(S["rr"], n_trl)], inc=S["rt"])

        def f_v(e):
            for t in range(4):
                ins = e.reduce_max(rcol[:, t:t + 1], lsb[:, t * 8:(t + 1) * 8],
                                   axis=AX.X, negate=True)
            return ins
        n_nm = P.op("vector", f_v, waits=[(S["rt"], n_log)], inc=S["rt"])

        def f_s(e):
            for t in range(4):
                ins = e.activation(expv[:, t * 8:(t + 1) * 8], lsb[:, t * 8:(t + 1) * 8],
                                   AF.Exp, bias=rcol[:, t:t + 1], scale=1.0)
            return ins
        n_exp = P.op("scalar", f_s, waits=[(S["rt"], n_nm)], inc=S["rs"])

        def f_v(e):
            for t in range(4):
                ins = e.reduce_sum(rcol[:, 4 + t:5 + t], expv[:, t * 8:(t + 1) * 8],
                                   axis=AX.X)
            return ins
        n_rs0 = P.op("vector", f_v, waits=[(S["rs"], n_exp)], inc=S["rt"])
        n_rsum = P.op("vector", lambda e: e.reciprocal(rcol[:, 8:12], rcol[:, 4:8]),
                      waits=[(S["rt"], n_rs0)], inc=S["rt"])

        def f_v(e):
            for t in range(4):
                e.tensor_scalar(probs[:, t * 8:(t + 1) * 8], expv[:, t * 8:(t + 1) * 8],
                                rcol[:, 8 + t:9 + t], None, ALU.mult)
            for t in range(4):
                ins = e.reduce_max(rcol[:, 12 + t:13 + t], probs[:, t * 8:(t + 1) * 8],
                                   axis=AX.X)
            return ins
        n_m1 = P.op("vector", f_v, waits=[(S["rt"], n_rsum)], inc=S["rt"])

        def f_v(e):
            for t in range(4):
                e.tensor_scalar(mskv[:, t * 8:(t + 1) * 8], probs[:, t * 8:(t + 1) * 8],
                                rcol[:, 12 + t:13 + t], -1e30, ALU.is_equal, ALU.mult)
            e.tensor_tensor(mskv[:], mskv[:], probs[:], ALU.add)
            for t in range(4):
                ins = e.reduce_max(rcol[:, 16 + t:17 + t], mskv[:, t * 8:(t + 1) * 8],
                                   axis=AX.X)
            return ins
        n_m2a = P.op("vector", f_v, waits=[(S["rt"], n_m1)], inc=S["rt"])
        n_m2 = P.op("vector", lambda e: e.tensor_scalar(rcol[:, 20:24], rcol[:, 16:20],
                                                        -2.0, None, ALU.mult),
                    waits=[(S["rt"], n_m2a)], inc=S["rt"])

        def f_s(e):
            for t in range(4):
                ins = e.activation(mskv[:, t * 8:(t + 1) * 8], probs[:, t * 8:(t + 1) * 8],
                                   AF.Sigmoid, bias=rcol[:, 20 + t:21 + t], scale=2.0)
            return ins
        n_sig = P.op("scalar", f_s, waits=[(S["rt"], n_m2)], inc=S["rs"])

        def f_v(e):
            e.tensor_tensor(expv[:], probs[:], mskv[:], ALU.mult)
            for t in range(4):
                ins = e.reduce_sum(rcol[:, 24 + t:25 + t], expv[:, t * 8:(t + 1) * 8],
                                   axis=AX.X)
            return ins
        n_rw0 = P.op("vector", f_v, waits=[(S["rs"], n_sig)], inc=S["rt"])

        def f_v(e):
            e.tensor_scalar(rcol[:, 24:28], rcol[:, 24:28], 1e-9, None, ALU.add)
            return e.reciprocal(rcol[:, 28:32], rcol[:, 24:28])
        n_rws = P.op("vector", f_v, waits=[(S["rt"], n_rw0)], inc=S["rt"])

        def f_v(e):
            for t in range(4):
                ins = e.tensor_scalar(wnb[:, t * 8:(t + 1) * 8], expv[:, t * 8:(t + 1) * 8],
                                      rcol[:, 28 + t:29 + t], None, ALU.mult)
            return ins
        n_wn = P.op("vector", f_v, waits=[(S["rt"], n_rws)], inc=S["rt"])

        def f_t(e):
            for t in range(4):
                ins = e.transpose(vtp[0:8, t * 128:(t + 1) * 128],
                                  wnb[:, t * 8:(t + 1) * 8], idb[:])
            return ins
        n_wt = P.op("tensor", f_t, waits=[(S["rt"], n_wn),
                                          (S["r1"], vte_done[NKV - 1])],
                    inc=S["rr"])

        n_wf = P.op("vector", lambda e: e.tensor_copy(we_b[:], vtp[0:8, 0:512]),
                    waits=[(S["rr"], n_wt)], inc=S["rt"])

        n_pm = P.op("tensor", lambda e: e.matmul(
            kvp[1][0][0:32, :], limeb[:], we_b[:], start=True, stop=True),
            waits=[(S["rt"], n_wf)], inc=S["rr"])

        def f_v(e):
            zpv = zpbf[:].rearrange("p (a two) -> p two a", two=2)
            zfv = zf[:].rearrange("p (a two) -> p two a", two=2)
            e.tensor_tensor(zpv[:, 0, :], zfv[:, 0, :], kvp[1][0][0:32, :], ALU.mult)
            return e.tensor_tensor(zpv[:, 1, :], zfv[:, 1, :], kvp[1][0][0:32, :],
                                   ALU.mult)
        n_zp = P.op("vector", f_v, waits=[(S["rr"], n_pm)], inc=S["rt"])

        if phases == 2:
            dumps = [
                (out_d[0:32, :], zpbf[:]), (out_d[32:64, :], zf[:]),
                (out_d[64:192, 0:32], lsb[:]), (out_d[192:320, 0:32], probs[:]),
                (out_d[320:448, 0:32], mskv[:]), (out_d[448:576, 0:32], rcol[:]),
                (out_d[576:584, 0:512], hslf[:]), (out_d[584:592, 0:512], dslf[:]),
                (out_d[592:593, 0:4], mxg[:]), (out_d[593:594, 0:16], tiny[:]),
                (out_d[728:856, 0:32], wnb[:]),
                (out_d[860:988, 0:32], expv[:]),
                (out_d[990:998, 0:32], limeb[:]),
            ]
            first = True
            for dd, ss_ in dumps:
                w = [(S["rt"], n_zp)] if first else []
                first = False
                P.op("gpsimd", lambda e, d=dd, s2=ss_: e.dma_start(out=d, in_=s2),
                     waits=w, inc=S["o4"], inck=16)
            P.wait("gpsimd", S["o4"], 16 * 13)
            return _finish(nc, P, blk, st)
        P.barrier_inc("vector", S["b23"])
        P.barrier_inc("tensor", S["b23"])
        P.barrier_inc("scalar", S["b23"])
        st1.close()

        st3 = st.enter_context(ExitStack())
        sb3 = lambda *a: st3.enter_context(nc.sbuf_tensor(*a))
        ksb = sb3("ksb", [128, 2 * 4096], BF16)
        vtsb = sb3("vtsb", [128, 2 * 4096], BF16)
        Pt = [sb3(f"Pt{i}", [128, TL], BF16) for i in range(2)]
        qring = [sb3(f"qring{i}", [128, TL], BF16) for i in range(4)]
        ndl = sb3("ndl", [1, TL], F32)
        dnt = sb3("dnt", [1, TL], F32)
        rbn = sb3("rbn", [128, TL], F32)
        ps3 = lambda n: st3.enter_context(nc.psum_tensor(n, [128, 512], F32))
        sp = [[ps3(f"sp{i}{j}") for j in range(2)] for i in range(2)]
        dn = [ps3(f"dn{j}") for j in range(2)]
        op = [ps3(f"op{j}") for j in range(2)]

        ts_cnt = {}
        se_cnt = {}
        tv_cnt = {}
        sl_cnt = {}
        tb_cnt = {}
        sr_cnt = {}
        vo_cnt = {}
        ql_cnt = {}
        kq_cnt = {}

        def ktile(kvh2, j):
            return ksb[:, (kvh2 * 32 + j) * 128:(kvh2 * 32 + j + 1) * 128]

        def vttile(kvh2, j):
            return vtsb[:, (kvh2 * 32 + j) * 128:(kvh2 * 32 + j + 1) * 128]

        def emit_boundary(hp):
            def f_v(e):
                e.tensor_copy(dnt[0:1, 0:512], dn[0][0:1, :])
                e.tensor_copy(dnt[0:1, 512:1024], dn[1][0:1, :])
                return e.reciprocal(ndl[:], dnt[:])
            sl_cnt[hp] = P.op("vector", f_v, waits=[(S["tv"], tv_cnt[(hp, 31)])],
                              inc=S["sl"])

            def f_t(e):
                e.matmul(dn[0][:], onesf[:], ndl[0:1, 0:512], start=True, stop=True,
                         skip_group_check=True)
                return e.matmul(dn[1][:], onesf[:], ndl[0:1, 512:1024], start=True,
                                stop=True, skip_group_check=True)
            tb_cnt[hp] = P.op("tensor", f_t, waits=[(S["sl"], sl_cnt[hp])],
                              inc=S["tb"])

            def f_s(e):
                e.activation(rbn[:, 0:512], dn[0][:], AF.Copy)
                return e.activation(rbn[:, 512:1024], dn[1][:], AF.Copy)
            wrb = [(S["tb"], tb_cnt[hp])]
            if hp >= 1:
                wrb.append((S["vo"], vo_cnt[hp - 1]))
            sr_cnt[hp] = P.op("scalar", f_s, waits=wrb, inc=S["sr"])

            def f_v(e):
                e.tensor_tensor(osb[:, hp * TL:hp * TL + 512], op[0][:],
                                rbn[:, 0:512], ALU.mult)
                return e.tensor_tensor(osb[:, hp * TL + 512:(hp + 1) * TL],
                                       op[1][:], rbn[:, 512:1024], ALU.mult)
            vo_cnt[hp] = P.op("vector", f_v, waits=[(S["sr"], sr_cnt[hp])],
                              inc=S["vo"])

        for qt in range(4):
            for kvh2 in range(2):
                kvh = 2 * qt + kvh2
                if qt == 0:
                    w = [(S["cc"], n_ag), (S["b23"], 3)] if kvh2 == 0 else []
                else:
                    lh = (qt - 1) * 8 + kvh2 * 4 + 3
                    w = [(S["ts"], ts_cnt[(lh, 31)]), (S["tv"], tv_cnt[(lh, 31)])]
                for r in range(4):
                    dstk = ksb[:, kvh2 * 4096:(kvh2 + 1) * 4096].rearrange(
                        "p (i f m) -> p i f m", i=8, f=4)[:, :, r, :]
                    srck = kvdst[r * 128:(r + 1) * 128,
                                 kvh * TL:(kvh + 1) * TL].rearrange(
                        "q (i m) -> q i m", i=8)
                    P.op("sync", lambda e, d=dstk, s=srck: e.dma_start(out=d, in_=s),
                         waits=w, inc=S["kq"], inck=16)
                    w = []
                    dstv = vtsb[:, kvh2 * 4096:(kvh2 + 1) * 4096].rearrange(
                        "p (i f m) -> p i f m", i=8, f=4)[:, :, r, :]
                    srcv = kvdst[r * 128:(r + 1) * 128, 8192:16384].rearrange(
                        "q (i c m) -> q i c m", i=8, c=8)[:, :, kvh, :]
                    P.op("sync", lambda e, d=dstv, s=srcv: e.dma_start(out=d, in_=s),
                         waits=w, inc=S["kq"], inck=16)
                kq_cnt[(qt, kvh2)] = S["kq"].n

            for hh in range(8):
                h = qt * 8 + hh
                kvh2 = hh // 4

                w = []
                if h == 0:
                    w.append((S["b23"], 3))
                if h >= 4:
                    w.append((S["ts"], ts_cnt[(h - 4, 31)]))
                ql_cnt[h] = P.op("gpsimd", lambda e, h=h: e.dma_start(
                    out=qring[h % 4][:], in_=q_dram[h][:, :]),
                    waits=w, inc=S["ql"], inck=16)

                qr = qring[h % 4]

                def emit_tv(h, j, kvh2):
                    jp = j % 2
                    lo = 128 * (j // 4)
                    w = [(S["se"], se_cnt[(h, j)])]
                    if j == 0 and h >= 1:
                        w.append((S["vo"], vo_cnt[h - 1]))

                    def f_t(e, j=j, jp=jp, lo=lo, kvh2=kvh2):
                        st_ = (j == 0)
                        vt = vttile(kvh2, j)
                        if lo < 512:
                            e.matmul(dn[0][0:1, lo:512], onesb[:], Pt[jp][:, lo:512],
                                     start=st_, stop=(j == 31), skip_group_check=True)
                            e.matmul(dn[1][0:1, :], onesb[:], Pt[jp][:, 512:1024],
                                     start=st_, stop=(j == 31), skip_group_check=True)
                            e.matmul(op[0][:, lo:512], vt, Pt[jp][:, lo:512],
                                     start=st_, stop=(j == 31), skip_group_check=True)
                            return e.matmul(op[1][:], vt, Pt[jp][:, 512:1024],
                                            start=st_, stop=(j == 31),
                                            skip_group_check=True)
                        e.matmul(dn[1][0:1, lo - 512:512], onesb[:], Pt[jp][:, lo:1024],
                                 start=False, stop=(j == 31), skip_group_check=True)
                        return e.matmul(op[1][:, lo - 512:512], vt, Pt[jp][:, lo:1024],
                                        start=False, stop=(j == 31),
                                        skip_group_check=True)
                    tv_cnt[(h, j)] = P.op("tensor", f_t, waits=w, inc=S["tv"])

                for j in range(32):
                    jp = j % 2
                    lo = 128 * (j // 4)
                    w = []
                    if j == 0:
                        w.append((S["ql"], ql_cnt[h]))
                        if hh == 0:
                            w.append((S["kq"], kq_cnt[(qt, 0)]))
                        if hh == 4:
                            w.append((S["kq"], kq_cnt[(qt, 1)]))
                        if h >= 1:
                            w.append((S["se"], se_cnt[(h - 1, 30)]))
                    elif j == 1:
                        if h >= 1:
                            w.append((S["se"], se_cnt[(h - 1, 31)]))
                    else:
                        w.append((S["se"], se_cnt[(h, j - 2)]))

                    def f_t(e, kvh2=kvh2, j=j, jp=jp, lo=lo, qr=qr):
                        kt = ktile(kvh2, j)
                        if lo < 512:
                            e.matmul(sp[jp][0][:, lo:512], kt, qr[:, lo:512],
                                     start=True, stop=True, skip_group_check=True)
                            e.matmul(sp[jp][1][:], kt, qr[:, 512:1024],
                                     start=True, stop=True, skip_group_check=True)
                            mt = sp[jp][0][:, lo:lo + 128]
                        else:
                            e.matmul(sp[jp][1][:, lo - 512:512], kt, qr[:, lo:1024],
                                     start=True, stop=True, skip_group_check=True)
                            mt = sp[jp][1][:, lo - 512:lo - 512 + 128]
                        return e.matmul(mt, masktb[:, (j % 4) * 128:(j % 4 + 1) * 128],
                                        idb[:], start=False, stop=True,
                                        skip_group_check=True)
                    ts_cnt[(h, j)] = P.op("tensor", f_t, waits=w, inc=S["ts"])

                    w = [(S["ts"], ts_cnt[(h, j)])]
                    if j >= 2:
                        w.append((S["tv"], tv_cnt[(h, j - 2)]))
                    if h == 0 and j == 0:
                        w.append((S["rt"], n_bcv))

                    def f_s(e, jp=jp, lo=lo):
                        if lo < 512:
                            e.activation(Pt[jp][:, lo:512], sp[jp][0][:, lo:512],
                                         AF.Exp, bias=negb[:], scale=SCALE)
                            return e.activation(Pt[jp][:, 512:1024], sp[jp][1][:],
                                                AF.Exp, bias=negb[:], scale=SCALE)
                        return e.activation(Pt[jp][:, lo:1024],
                                            sp[jp][1][:, lo - 512:512],
                                            AF.Exp, bias=negb[:], scale=SCALE)
                    se_cnt[(h, j)] = P.op("scalar", f_s, waits=w, inc=S["se"])

                    if j == 0 and h >= 1:
                        emit_boundary(h - 1)

                    if j >= 1:
                        emit_tv(h, j - 1, kvh2)

                emit_tv(h, 31, kvh2)

        emit_boundary(31)

        if phases == 3:
            P.op("gpsimd", lambda e: e.dma_start(out=out_d[0:128, :],
                                                 in_=osb[:, 31 * TL:32 * TL]),
                 waits=[(S["vo"], vo_cnt[31])], inc=S["o4"], inck=16)
            P.wait("gpsimd", S["o4"], 16)
            return _finish(nc, P, blk, st)
        P.barrier_inc("vector", S["b34"])
        P.barrier_inc("tensor", S["b34"])
        P.barrier_inc("scalar", S["b34"])
        st3.close()

        st4 = st.enter_context(ExitStack())
        sb4 = lambda *a: st4.enter_context(nc.sbuf_tensor(*a))
        wf32e = sb4("wf32e", [128, D], F32)
        wobf = sb4("wobf", [128, D], BF16)
        wbfe = sb4("wbfe", [128, D], BF16)
        xoc = [sb4(f"xoc{i}", [128, TL], F32) for i in range(2)]
        outsb = [sb4(f"outsb{i}", [128, TL], F32) for i in range(2)]
        gsb = [sb4(f"gsb{i}", [128, TL], F32) for i in range(2)]
        tmp4 = sb4("tmp4", [128, TL], F32)
        ps4 = lambda n: st4.enter_context(nc.psum_tensor(n, [128, 512], F32))
        sP = [[ps4(f"sP{i}{j}") for j in range(2)] for i in range(2)]
        fP = [[ps4(f"fP{i}{j}") for j in range(2)] for i in range(2)]

        t4s = {}
        t4f = {}
        s4g = {}
        v4c = {}
        wc4 = {}
        x4c = {}

        def emit_tail4(oc):
            p = oc % 2

            def f_s(e, p=p, oc=oc):
                e.activation(gsb[p][:, 0:512], fP[p][0][:], AF.Silu,
                             bias=bsb[:, oc:oc + 1], scale=1.0)
                return e.activation(gsb[p][:, 512:1024], fP[p][1][:], AF.Silu,
                                    bias=bsb[:, oc:oc + 1], scale=1.0)
            wsl = [(S["t4"], t4f[oc])]
            if oc >= 2:
                wsl.append((S["v4"], v4c[oc - 2]))
            s4g[oc] = P.op("scalar", f_s, waits=wsl, inc=S["s4"])

            w = [(S["s4"], s4g[oc]), (S["t4"], t4s[oc]), (S["x4"], x4c[oc])]
            if oc >= 2:
                w.append((S["o4"], 16 * (oc - 1)))

            def f_v(e, p=p):
                e.tensor_tensor(tmp4[:, 0:512], sP[p][0][:], gsb[p][:, 0:512], ALU.mult)
                e.tensor_tensor(tmp4[:, 512:1024], sP[p][1][:], gsb[p][:, 512:1024],
                                ALU.mult)
                e.tensor_tensor(outsb[p][:, 0:512], tmp4[:, 0:512], xoc[p][:, 0:512],
                                ALU.add)
                return e.tensor_tensor(outsb[p][:, 512:1024], tmp4[:, 512:1024],
                                       xoc[p][:, 512:1024], ALU.add)
            v4c[oc] = P.op("vector", f_v, waits=w, inc=S["v4"])

            P.op("gpsimd", lambda e, p=p, oc=oc: e.dma_start(
                out=out_d[oc * 128:(oc + 1) * 128, :], in_=outsb[p][:]),
                waits=[(S["v4"], v4c[oc])], inc=S["o4"], inck=16)

        for oc in range(DC):
            p = oc % 2
            w = [(S["b34"], 3)] if oc == 0 else [(S["wc4"], 2 * oc)]
            P.op("sync", lambda e, oc=oc: e.dma_start(out=wf32e[:], in_=wo_d[oc][:, :]),
                 waits=w, inc=S["wl4"], inck=16)
            P.op("sync", lambda e, oc=oc: e.dma_start(out=wf32e[:], in_=w_d[oc][:, :]),
                 waits=[(S["wc4"], 2 * oc + 1)], inc=S["wl4"], inck=16)

            w = [(S["wl4"], 16 * (2 * oc + 1))]
            if oc >= 1:
                w.append((S["t4"], t4s[oc - 1]))
            wc4[2 * oc] = P.op("scalar", lambda e: e.activation(wobf[:], wf32e[:],
                                                                AF.Copy),
                               waits=w, inc=S["wc4"])
            w = [(S["wl4"], 16 * (2 * oc + 2))]
            if oc >= 1:
                w.append((S["t4"], t4f[oc - 1]))
            wc4[2 * oc + 1] = P.op("scalar", lambda e: e.activation(wbfe[:], wf32e[:],
                                                                    AF.Copy),
                                   waits=w, inc=S["wc4"])

            w = []
            if oc == 0:
                w.append((S["b34"], 3))
            if oc >= 2:
                w.append((S["v4"], v4c[oc - 2]))
            x4c[oc] = P.op("gpsimd", lambda e, p=p, oc=oc: e.dma_start(
                out=xoc[p][:], in_=x_d[oc * 128:(oc + 1) * 128, :]),
                waits=w, inc=S["x4"], inck=16)

            if oc >= 1:
                emit_tail4(oc - 1)

            w = [(S["wc4"], wc4[2 * oc])]
            if oc == 0:
                w.append((S["vo"], vo_cnt[31]))
            if oc >= 2:
                w.append((S["v4"], v4c[oc - 2]))

            def f_t(e, p=p):
                for hc in range(H):
                    st_, sp_ = (hc == 0), (hc == H - 1)
                    lw = wobf[:, hc * 128:(hc + 1) * 128]
                    e.matmul(sP[p][0][:], lw, osb[:, hc * TL:hc * TL + 512],
                             start=st_, stop=sp_)
                    ins = e.matmul(sP[p][1][:], lw, osb[:, hc * TL + 512:(hc + 1) * TL],
                                   start=st_, stop=sp_)
                return ins
            t4s[oc] = P.op("tensor", f_t, waits=w, inc=S["t4"])

            w = [(S["wc4"], wc4[2 * oc + 1])]
            if oc == 0:
                w.append((S["rt"], n_zp))
            if oc >= 2:
                w.append((S["s4"], s4g[oc - 2]))

            def f_t(e, p=p, oc=oc):
                for dc in range(DC):
                    st_ = (dc == 0)
                    lw = wbfe[:, dc * 128:(dc + 1) * 128]
                    e.matmul(fP[p][0][:], lw, xn[:, dc * TL:dc * TL + 512],
                             start=st_, stop=False, skip_group_check=True)
                    e.matmul(fP[p][1][:], lw, xn[:, dc * TL + 512:(dc + 1) * TL],
                             start=st_, stop=False, skip_group_check=True)
                lb = bmb[:, oc * 128:(oc + 1) * 128]
                e.matmul(fP[p][0][:], lb, zpbf[:, 0:512], start=False, stop=True,
                         skip_group_check=True)
                return e.matmul(fP[p][1][:], lb, zpbf[:, 512:1024], start=False,
                                stop=True, skip_group_check=True)
            t4f[oc] = P.op("tensor", f_t, waits=w, inc=S["t4"])

        emit_tail4(DC - 1)

        P.wait("gpsimd", S["o4"], 16 * DC)

        _emit_all(P, blk)
        st4.close()
    return nc


def _emit_all(P, blk):
    def emit(eng_name):
        def body(e):
            for fn in P.q[eng_name]:
                fn(e)
        return body
    blk.sync(emit("sync"))
    blk.vector(emit("vector"))
    blk.scalar(emit("scalar"))
    blk.tensor(emit("tensor"))
    blk.gpsimd(emit("gpsimd"))


def _finish(nc, P, blk, st):
    _emit_all(P, blk)
    return nc



def _llama3_inv_freq():
    base = 500000.0
    inv_freq = 1.0 / base ** (np.arange(0, HD, 2, dtype=np.float32) / HD)
    factor, low_f, high_f, old_ctx = 32.0, 1.0, 4.0, 8192
    low_wl, high_wl = old_ctx / low_f, old_ctx / high_f
    wavelen = 2 * math.pi / inv_freq
    inv_llama = np.where(wavelen > low_wl, inv_freq / factor, inv_freq)
    smooth = (old_ctx / wavelen - low_f) / (high_f - low_f)
    smoothed = (1 - smooth) * inv_llama / factor + smooth * inv_llama
    is_med = ~(wavelen < high_wl) & ~(wavelen > low_wl)
    return np.where(is_med, smoothed, inv_llama).astype(np.float32)


_NC_CACHE = {}


def _get_nc():
    import os
    ph = int(os.environ.get("MK_PHASES", "4"))
    key = ("nc", ph)
    if key not in _NC_CACHE:
        _NC_CACHE[key] = build_program(ph)
    return _NC_CACHE[key]


def kernel(L0_attn_x, L0_attn_Wq, L0_attn_Wk, L0_attn_Wv, L0_attn_Wo,
           L0_ffn_W, L0_ffn_b, L0_ffn_A, L0_ffn_Bmat, L0_ffn_LiMEs):
    x = np.ascontiguousarray(np.asarray(L0_attn_x, dtype=np.float32))
    Wq = np.asarray(L0_attn_Wq, dtype=np.float32)
    Wk = np.asarray(L0_attn_Wk, dtype=np.float32)
    Wv = np.asarray(L0_attn_Wv, dtype=np.float32)
    Wo = np.asarray(L0_attn_Wo, dtype=np.float32)
    W = np.asarray(L0_ffn_W, dtype=np.float32)
    bvec = np.asarray(L0_ffn_b, dtype=np.float32)
    A = np.asarray(L0_ffn_A, dtype=np.float32)
    Bmat = np.asarray(L0_ffn_Bmat, dtype=np.float32)
    LiMEs = np.asarray(L0_ffn_LiMEs, dtype=np.float32)

    def stage_blocks(Wm, nblk):
        return np.ascontiguousarray(
            Wm.reshape(nblk, 128, DC, 128).transpose(0, 3, 2, 1).reshape(nblk, 128, D))

    wq_s = stage_blocks(Wq, H)
    wk_s = stage_blocks(Wk, KVH)
    wv_s = stage_blocks(Wv, KVH)
    wo_s = stage_blocks(Wo, DC)
    w_s = stage_blocks(W, DC)
    w8_s = np.ascontiguousarray(
        W[0:8].reshape(8, DC, 128).transpose(2, 1, 0).reshape(128, DC * 8))
    a_s = np.ascontiguousarray(
        A.reshape(DC, 128, 32).transpose(1, 0, 2).reshape(128, DC * 32))
    limt_s = np.ascontiguousarray(LiMEs.T)
    b_s = np.ascontiguousarray(bvec.reshape(DC, 128).T)
    b8_s = np.ascontiguousarray(bvec[0:8].reshape(8, 1))

    idb = np.eye(128, dtype=ml_dtypes.bfloat16)
    idf = np.eye(8, dtype=np.float32)
    onesb = np.ones((128, 1), dtype=ml_dtypes.bfloat16)
    onesf = np.ones((1, 128), dtype=np.float32)
    perm = np.zeros((128, 128), dtype=np.float32)
    for m in range(64):
        perm[m + 64, m] = -1.0
        perm[m, m + 64] = 1.0
    perm = perm.astype(ml_dtypes.bfloat16)

    inv = _llama3_inv_freq()

    in_maps = []
    token_ids = {}
    for c in range(NCORE):
        b, r = c // 4, c % 4
        tid = np.concatenate([np.arange(128 * (4 * i + r), 128 * (4 * i + r) + 128)
                              for i in range(8)])
        token_ids[c] = (b, tid)
        xs = np.ascontiguousarray(x[b][tid].T)

        pos = tid.astype(np.float32)
        ang = pos[None, :] * inv[:, None]
        cosl = np.ascontiguousarray(np.concatenate([np.cos(ang), np.cos(ang)], axis=0)
                                    .astype(ml_dtypes.bfloat16))
        sinl = np.ascontiguousarray(np.concatenate([np.sin(ang), np.sin(ang)], axis=0)
                                    .astype(ml_dtypes.bfloat16))

        mt = np.zeros((128, 512), dtype=np.float32)
        qt_i = np.arange(128)[:, None]
        kt_i = np.arange(128)[None, :]
        for jm in range(4):
            tile = mt[:, jm * 128:(jm + 1) * 128]
            if jm == r:
                tile[kt_i > qt_i] = NEGBIG
            elif jm > r:
                tile[:, :] = NEGBIG
        mtb = mt.astype(ml_dtypes.bfloat16)

        in_maps.append({
            "x": xs, "wq": wq_s, "wk": wk_s, "wv": wv_s, "wo": wo_s, "wff": w_s,
            "w8": w8_s, "amat": a_s, "bmat": Bmat, "lime": LiMEs, "limt": limt_s,
            "bvec": b_s, "b8": b8_s, "cosl": cosl, "sinl": sinl, "permb": perm,
            "identb": idb, "identf": idf, "masktb": mtb, "onesb": onesb,
            "onesf": onesf,
        })

    nc = _get_nc()
    res = run_bass_kernel_spmd(nc, in_maps, core_ids=list(range(NCORE)))

    out = np.empty((B, T, D), dtype=np.float32)
    for c in range(NCORE):
        b, tid = token_ids[c]
        out[b, tid, :] = res.results[c]["out"].T
    return out


if __name__ == "__main__":
    import sys
    if len(sys.argv) > 1 and sys.argv[1] == "build":
        nc = build_program()
        n_inst = sum(1 for _ in nc.all_instructions())
        print(f"built ok: {n_inst} instructions")
